# version 12
# baseline (speedup 1.0000x reference)
import numpy as np
import concourse.bass as bass
import concourse.mybir as mybir
from concourse.bass_utils import run_bass_kernel_spmd

F32 = mybir.dt.float32
BF16 = mybir.dt.bfloat16
AF = mybir.ActivationFunctionType
ALU = mybir.AluOpType
AXX = mybir.AxisListType.X

D = 1024
NT = 18
NT_OWN = 16
TOK = NT * 128
HALO = 256
NH = 16
HD = 64
NE = 18
NEG = -1.0e30


class P:
    def __init__(self):
        self.ops = {"pe": [], "dv": [], "ac": [], "gp": [], "sy": []}
        self.cnt = {}
        self.waited = {e: {} for e in self.ops}

    def count(self, sem):
        return self.cnt.get(sem, 0)

    def raw(self, eng, fn):
        self.ops[eng].append(fn)

    def wait(self, eng, sem, val):
        if val is None or val <= 0:
            return
        assert val <= self.count(sem), f"wait {sem}>={val} > emitted {self.count(sem)}"
        if val > self.waited[eng].get(sem, 0):
            self.waited[eng][sem] = val
            self.ops[eng].append(("w", sem, val))

    def inc(self, eng, sem, fn, n=1):
        self.cnt[sem] = self.count(sem) + n
        self.ops[eng].append(("i", sem, n, fn))
        return self.cnt[sem]


def build_program(phases=7):
    nc = bass.Bass("TRN2", target_bir_lowering=False)

    xin = nc.dram_tensor("xin", [TOK, D], F32, kind="ExternalInput").ap()
    masks = nc.dram_tensor("masks", [3, 128, 384], F32, kind="ExternalInput").ap()
    cmask = nc.dram_tensor("cmask", [128, 544], F32, kind="ExternalInput").ap()
    identw = nc.dram_tensor("identw", [128, 128], F32, kind="ExternalInput").ap()
    bqkv = nc.dram_tensor("bqkv", [128, 24], F32, kind="ExternalInput").ap()
    rbias_b = nc.dram_tensor("rbias_b", [128, 16], F32, kind="ExternalInput").ap()
    onesw = nc.dram_tensor("onesw", [1, 512], F32, kind="ExternalInput").ap()
    rw = nc.dram_tensor("rw", [16, D], F32, kind="ExternalInput").ap()
    bo_w = nc.dram_tensor("bo_w", [1, D], F32, kind="ExternalInput").ap()
    wq_w = nc.dram_tensor("wq_w", [128, 8192], F32, kind="ExternalInput").ap()
    wk_w = nc.dram_tensor("wk_w", [128, 8192], F32, kind="ExternalInput").ap()
    wv_w = nc.dram_tensor("wv_w", [128, 8192], F32, kind="ExternalInput").ap()
    wo_w = nc.dram_tensor("wo_w", [128, 8192], F32, kind="ExternalInput").ap()
    gate_w = nc.dram_tensor("gate_w", [16, 128, 8192], F32, kind="ExternalInput").ap()
    up_w = nc.dram_tensor("up_w", [16, 128, 8192], F32, kind="ExternalInput").ap()
    down_w = nc.dram_tensor("down_w", [16, 128, 8192], F32, kind="ExternalInput").ap()
    shg_w = nc.dram_tensor("shg_w", [2, 128, 8192], F32, kind="ExternalInput").ap()
    shu_w = nc.dram_tensor("shu_w", [2, 128, 8192], F32, kind="ExternalInput").ap()
    shd_w = nc.dram_tensor("shd_w", [2, 128, 8192], F32, kind="ExternalInput").ap()
    out = nc.dram_tensor("out", [NT_OWN * 128, D], F32, kind="ExternalOutput").ap()

    import contextlib
    with contextlib.ExitStack() as _ctx:
        x_sb = _ctx.enter_context(nc.sbuf_tensor("x_sb", [128, NT * D], F32))
        xn_sb = _ctx.enter_context(nc.sbuf_tensor("xn_sb", [128, 8 * TOK], BF16))
        wslot = _ctx.enter_context(nc.sbuf_tensor("wslot", [128, 3 * 8192], BF16))
        hu_sb = _ctx.enter_context(nc.sbuf_tensor("hu_sb", [128, 8 * 512], BF16))
        gs_sb = _ctx.enter_context(nc.sbuf_tensor("gs_sb", [128, 1024], F32))
        comb_sb = _ctx.enter_context(nc.sbuf_tensor("comb_sb", [128, NT * NE], F32))
        st_sb = _ctx.enter_context(nc.sbuf_tensor("st_sb", [128, 64], F32))
        arena = _ctx.enter_context(nc.sbuf_tensor("arena", [128, 8192], BF16))
        q_sb = _ctx.enter_context(nc.sbuf_tensor("q_sb", [128, 1024], BF16))
        pr32 = _ctx.enter_context(nc.sbuf_tensor("pr32", [128, 768], F32))
        prT = _ctx.enter_context(nc.sbuf_tensor("prT", [128, 768], BF16))
        o_sb = _ctx.enter_context(nc.sbuf_tensor("o_sb", [128, 1024], BF16))
        mask_sb = _ctx.enter_context(nc.sbuf_tensor("mask_sb", [128, 3 * 384], F32))
        cm_sb = _ctx.enter_context(nc.sbuf_tensor("cm_sb", [128, 544], F32))
        ident = _ctx.enter_context(nc.sbuf_tensor("ident", [128, 128], F32))
        ones_bf = _ctx.enter_context(nc.sbuf_tensor("ones_bf", [1, 512], BF16))
        bo_sb = _ctx.enter_context(nc.sbuf_tensor("bo_sb", [1, D], BF16))
        bcol = _ctx.enter_context(nc.sbuf_tensor("bcol", [128, 24], F32))
        rT_sb = _ctx.enter_context(nc.sbuf_tensor("rT_sb", [128, 128], BF16))
        rb_sb = _ctx.enter_context(nc.sbuf_tensor("rb_sb", [128, 16], F32))
        xn_tm = _ctx.enter_context(nc.sbuf_tensor("xn_tm", [128, 1024], BF16))
        pA = _ctx.enter_context(nc.psum_tensor("pA", [128, 512], F32))
        pB = _ctx.enter_context(nc.psum_tensor("pB", [128, 512], F32))
        pC = _ctx.enter_context(nc.psum_tensor("pC", [128, 512], F32))
        pD = _ctx.enter_context(nc.psum_tensor("pD", [128, 512], F32))
        pE = _ctx.enter_context(nc.psum_tensor("pE", [128, 512], F32))
        pF = _ctx.enter_context(nc.psum_tensor("pF", [128, 512], F32))
        pG = _ctx.enter_context(nc.psum_tensor("pG", [128, 512], F32))
        pH = _ctx.enter_context(nc.psum_tensor("pH", [128, 512], F32))
        sx = _ctx.enter_context(nc.semaphore("sx"))
        sw = _ctx.enter_context(nc.semaphore("sw"))
        spe = _ctx.enter_context(nc.semaphore("spe"))
        sdv = _ctx.enter_context(nc.semaphore("sdv"))
        sac = _ctx.enter_context(nc.semaphore("sac"))
        sst = _ctx.enter_context(nc.semaphore("sst"))
        sxa = _ctx.enter_context(nc.semaphore("sxa"))
        block = _ctx.enter_context(nc.Block())
        SEM = {"sx": sx, "sw": sw, "spe": spe, "sdv": sdv, "sac": sac, "sst": sst, "sxa": sxa}
        p = P()

        rt_sb = arena[:].bitcast(F32)

        def kr(c, r):
            return arena[:, 512 * c + 128 * r: 512 * c + 128 * (r + 1)]

        def vr(r):
            return arena[:, 4096 + 1024 * r: 4096 + 1024 * (r + 1)]

        huF32 = hu_sb[:].bitcast(F32)

        def sc_buf(rho):
            if rho < 2:
                return gs_sb[:, 512 * rho: 512 * rho + 384]
            return huF32[:, 0:384]

        def pr_buf(rho):
            if rho < 2:
                return pr32[:, 384 * rho: 384 * rho + 384]
            return huF32[:, 384: 768]

        def pt_buf(rho):
            if rho < 2:
                return prT[:, 384 * rho: 384 * rho + 384]
            return hu_sb[:, 1536: 1920]

        NMX = {0: 54, 1: 55, 2: 50}
        DEN = {0: 56, 1: 57, 2: 51}
        RDE = {0: 58, 1: 59, 2: 52}
        BKS = {0: "E", 1: "F", 2: "D"}

        def xt(t):
            return x_sb[:, D * t: D * (t + 1)]

        def xnc(k, c0, c1):
            return xn_sb[:, TOK * k + c0: TOK * k + c1]

        def slot_chunk(s, k, f0, f1):
            return wslot[:, 8192 * s + 1024 * k + f0: 8192 * s + 1024 * k + f1]

        def wsrc(e):
            if e < 16:
                return gate_w[e], up_w[e], down_w[e]
            h = e - 16
            return shg_w[h], shu_w[h], shd_w[h]

        x_w = {t: ("sx", None) for t in range(NT)}
        x_r_ac = {t: 0 for t in range(NT)}
        bank_free = {b: ("spe", 0) for b in "ABCDEFGH"}
        PS = {"A": pA, "B": pB, "C": pC, "D": pD, "E": pE, "F": pF, "G": pG, "H": pH}

        xin3 = xin.rearrange("(t p) d -> t p d", p=128)
        for i in range(3):
            x_in_cnt = p.inc("sy", "sx", lambda sy, i=i: sy.dma_start(
                out=x_sb[:, D * 6 * i: D * 6 * (i + 1)].rearrange("p (t d) -> p t d", t=6),
                in_=xin3[6 * i: 6 * (i + 1)].transpose([1, 0, 2])), 16)
            for t in range(6 * i, 6 * (i + 1)):
                x_w[t] = ("sx", x_in_cnt)
        p.inc("sy", "sx", lambda sy: sy.dma_start(
            out=mask_sb[:].rearrange("p (v n) -> p v n", v=3),
            in_=masks.transpose([1, 0, 2])), 16)
        p.inc("sy", "sx", lambda sy: sy.dma_start(out=cm_sb[:], in_=cmask), 16)
        p.inc("sy", "sx", lambda sy: sy.dma_start(out=ident[:], in_=identw), 16)
        p.inc("sy", "sx", lambda sy: sy.dma_start(out=bcol[:], in_=bqkv), 16)
        p.inc("sy", "sx", lambda sy: sy.dma_start(out=rb_sb[:], in_=rbias_b), 16)
        rw_cnt = p.inc("sy", "sx", lambda sy: sy.dma_start(out=gs_sb[0:16, :], in_=rw), 16)
        init_cnt = p.count("sx")

        p.inc("gp", "sw", lambda g: g.dma_start(out=ones_bf[:], in_=onesw), 16)
        p.inc("gp", "sw", lambda g: g.dma_start(out=bo_sb[:], in_=bo_w), 16)

        p.wait("pe", "sx", rw_cnt)
        for c in range(8):
            rtr_cnt = p.inc("pe", "spe", lambda t, c=c: t.transpose(
                pF[:, 16 * c: 16 * (c + 1)], gs_sb[0:16, 128 * c: 128 * (c + 1)],
                ident[0:16, 0:16]))
        p.wait("ac", "spe", rtr_cnt)
        rT_cnt = p.inc("ac", "sac", lambda s: s.copy(rT_sb[:], pF[:, 0:128]))
        bank_free["F"] = ("sac", rT_cnt)
        comb3 = comb_sb[:].rearrange("p (t c) -> p t c", c=NE)
        comb_half_cnt = p.inc("dv", "sdv", lambda v: v.memset(comb3[:, :, 16:18], 0.5))

        slot_ready = [0, 0, 0]
        slot_half_ready = [[0, 0], [0, 0], [0, 0]]
        slot_free = [0, 0, 0]
        gs_free_ac = {0: 0, 1: 0}
        gs_w_ac = {0: 0, 1: 0}

        def load_matrix(slot, src2, war_pe):
            p.wait("gp", "spe", war_pe)
            for h in range(2):
                cnt = p.inc("gp", "sw", lambda g, src2=src2, h=h, slot=slot: g.dma_start(
                    out=wslot[:, 8192 * slot + 4096 * h: 8192 * slot + 4096 * (h + 1)],
                    in_=src2[:, 4096 * h: 4096 * (h + 1)]), 16)
                slot_half_ready[slot][h] = cnt
            slot_ready[slot] = cnt

        def emit_stats_batch(tiles):
            lo, hi = tiles[0], tiles[-1] + 1
            p.wait("ac", "sdv", gs_free_ac[0])
            p.wait("ac", "sdv", gs_free_ac[1])
            p.wait("ac", "sdv", ln_ts_last[0])
            p.wait("dv", "sac", ln_last_rstd[0])
            for t in tiles:
                xs, xcnt = x_w[t]
                p.wait("dv", xs, xcnt)
                p.inc("dv", "sdv", lambda v, t=t: v.reduce_sum(
                    st_sb[:, t: t + 1], xt(t), axis=AXX))
                p.wait("ac", xs, xcnt)
                sq_cnt = p.inc("ac", "sac", lambda s, t=t: s.activation(
                    gs_sb[:, 0:1024], xt(t), AF.Square,
                    accum_out=st_sb[:, 18 + t: 19 + t]))
                x_r_ac[t] = sq_cnt
            rs_all = p.count("sdv")
            p.wait("dv", "sdv", rs_all)
            p.wait("dv", "sac", sq_cnt)
            V1 = st_sb[:, lo: hi]
            V2 = st_sb[:, 18 + lo: 18 + hi]
            MM = st_sb[:, 36 + lo: 36 + hi]
            p.inc("dv", "sdv", lambda v: v.tensor_scalar(V1, V1, 1.0 / D, None, ALU.mult))
            p.inc("dv", "sdv", lambda v: v.tensor_scalar(V2, V2, 1.0 / D, None, ALU.mult))
            p.wait("dv", "sdv", p.count("sdv"))
            p.inc("dv", "sdv", lambda v: v.tensor_mul(MM, V1, V1))
            p.wait("dv", "sdv", p.count("sdv"))
            var_cnt = p.inc("dv", "sdv", lambda v: v.scalar_tensor_tensor(
                V2, V2, 1e-5, MM, ALU.add, ALU.subtract))
            p.wait("ac", "sdv", var_cnt)
            p.inc("ac", "sac", lambda s: s.activation(V2, V2, AF.Ln))
            p.wait("ac", "sac", p.count("sac"))
            rstd_cnt = p.inc("ac", "sac", lambda s: s.activation(V2, V2, AF.Exp, scale=-0.5))
            ln_last_rstd[0] = rstd_cnt
            return rstd_cnt

        def emit_xn_tile(t, xn_dst, stats_sac, queues=("sy", "ac"),
                         war_pe=0):
            p.wait("dv", "sac", stats_sac)
            xs, xcnt = x_w[t]
            p.wait("dv", xs, xcnt)
            p.wait("dv", "sx", xn_tm_free[0][0])
            p.wait("dv", "sxa", xn_tm_free[0][1])
            xn_cnt = p.inc("dv", "sdv", lambda v, t=t: v.tensor_scalar(
                xn_tm[:], xt(t), st_sb[:, t: t + 1], st_sb[:, 18 + t: 19 + t],
                ALU.subtract, ALU.mult))
            ln_ts_last[0] = xn_cnt
            for qi, q in enumerate(queues):
                p.wait(q, "sdv", xn_cnt)
                p.wait(q, "spe", war_pe)
            qsem = {"sy": "sx", "ac": "sxa"}
            for c in range(8):
                q = queues[c % len(queues)]
                p.inc(q, qsem[q], lambda e, c=c, t=t: e.dma_start(
                    out=xn_dst(c, t), in_=xn_tm[:, 128 * c: 128 * (c + 1)],
                    transpose=True), 16)
            ev_cnt = (p.count("sx"), p.count("sxa"))
            xn_tm_free[0] = ev_cnt
            return ev_cnt

        def emit_routing(t, ev_cnt):
            ro = 768 * (t % 2)
            def rt(x, y):
                return rt_sb[:, ro + x: ro + y]
            bs, bv = bank_free["F"]
            p.wait("pe", bs, bv)
            p.wait("pe", "sac", rT_cnt)
            p.wait("pe", "sx", ev_cnt[0])
            p.wait("pe", "sxa", ev_cnt[1])
            for k in range(8):
                lg = lambda t_, k=k, t=t: t_.matmul(
                    pF[:, 0:16], xnc(k, 128 * t, 128 * (t + 1)),
                    rT_sb[:, 16 * k: 16 * (k + 1)], start=(k == 0), stop=(k == 7))
                if k == 7:
                    lg_cnt = p.inc("pe", "spe", lg)
                else:
                    p.raw("pe", ("r", lg))
            p.wait("ac", "spe", lg_cnt)
            p.wait("ac", "spe", ring_free_pe[0])
            p.wait("ac", "sdv", rt_score_read[t % 2])
            texp_cnt = p.inc("ac", "sac", lambda s: s.activation(
                rt(16, 32), pF[:, 0:16], AF.Exp, scale=-1.0))
            bank_free["F"] = ("sac", texp_cnt)
            def dv(fn):
                p.wait("dv", "sdv", p.count("sdv"))
                return p.inc("dv", "sdv", fn)
            p.wait("dv", "sac", texp_cnt)
            dv(lambda v: v.tensor_scalar(rt(0, 16), rt(16, 32), 1.0, None, ALU.add))
            dv(lambda v: v.reciprocal(rt(16, 32), rt(0, 16)))
            p.wait("dv", "sx", init_cnt)
            dv(lambda v: v.tensor_add(rt(32, 48), rt(16, 32), rb_sb[:]))
            sg = rt(32, 48).rearrange("p (g i) -> p g i", g=4)
            dv(lambda v: v.tensor_add(rt(48, 60).rearrange("p (g i) -> p g i", g=4),
                                      sg[:, :, 0:3], sg[:, :, 1:4]))
            dv(lambda v: v.tensor_add(rt(60, 68).rearrange("p (g i) -> p g i", g=4),
                                      sg[:, :, 0:2], sg[:, :, 2:4]))
            dv(lambda v: v.tensor_add(rt(68, 72).rearrange("p (g i) -> p g i", g=4),
                                      sg[:, :, 0:1], sg[:, :, 3:4]))
            dv(lambda v: v.reduce_max(rt(72, 76).unsqueeze(2),
                                      rt(48, 60).rearrange("p (g i) -> p g i", g=4), axis=AXX))
            dv(lambda v: v.reduce_max(rt(76, 80).unsqueeze(2),
                                      rt(60, 68).rearrange("p (g i) -> p g i", g=4), axis=AXX))
            dv(lambda v: v.tensor_max(rt(72, 76), rt(72, 76), rt(76, 80)))
            dv(lambda v: v.tensor_max(rt(80, 84), rt(72, 76), rt(68, 72)))
            g2 = rt(80, 84)
            A4 = g2.unsqueeze(1).broadcast_to([128, 4, 4])
            B4 = g2.unsqueeze(2).broadcast_to([128, 4, 4])
            X4 = rt(84, 100).rearrange("p (i j) -> p i j", i=4)
            Y4 = rt(100, 116).rearrange("p (i j) -> p i j", i=4)
            dv(lambda v: v.tensor_tensor(X4, A4, B4, ALU.is_ge))
            dv(lambda v: v.tensor_tensor(Y4, A4, B4, ALU.is_gt))
            L4 = cm_sb[:, 0:16].rearrange("p (i j) -> p i j", i=4)
            U4 = cm_sb[:, 16:32].rearrange("p (i j) -> p i j", i=4)
            p.wait("dv", "sx", init_cnt)
            dv(lambda v: v.tensor_tensor(X4, X4, L4, ALU.mult))
            dv(lambda v: v.tensor_tensor(Y4, Y4, U4, ALU.mult))
            dv(lambda v: v.tensor_tensor(X4, X4, Y4, ALU.add))
            dv(lambda v: v.reduce_sum(rt(116, 120).unsqueeze(2), X4, axis=AXX))
            dv(lambda v: v.tensor_scalar(rt(120, 124), rt(116, 120),
                                         2.0, None, ALU.is_lt))
            em = rt(120, 124).unsqueeze(2).broadcast_to([128, 4, 4])
            dv(lambda v: v.tensor_tensor(
                rt(124, 140).rearrange("p (g i) -> p g i", g=4), em, em, ALU.max))
            dv(lambda v: v.tensor_scalar(rt(140, 156), rt(32, 48),
                                         1.0, None, ALU.add))
            dv(lambda v: v.tensor_mul(rt(140, 156), rt(140, 156),
                                      rt(124, 140)))
            dv(lambda v: v.tensor_scalar(rt(140, 156), rt(140, 156),
                                         1.0, None, ALU.subtract))
            mk = rt(140, 156)
            A16 = mk.unsqueeze(1).broadcast_to([128, 16, 16])
            B16 = mk.unsqueeze(2).broadcast_to([128, 16, 16])
            X16 = rt(160, 416).rearrange("p (i j) -> p i j", i=16)
            Y16 = rt(416, 672).rearrange("p (i j) -> p i j", i=16)
            dv(lambda v: v.tensor_tensor(X16, A16, B16, ALU.is_ge))
            dv(lambda v: v.tensor_tensor(Y16, A16, B16, ALU.is_gt))
            L16 = cm_sb[:, 32:288].rearrange("p (i j) -> p i j", i=16)
            U16 = cm_sb[:, 288:544].rearrange("p (i j) -> p i j", i=16)
            dv(lambda v: v.tensor_tensor(X16, X16, L16, ALU.mult))
            dv(lambda v: v.tensor_tensor(Y16, Y16, U16, ALU.mult))
            dv(lambda v: v.tensor_tensor(X16, X16, Y16, ALU.add))
            dv(lambda v: v.reduce_sum(rt(688, 704).unsqueeze(2), X16, axis=AXX))
            dv(lambda v: v.tensor_scalar(rt(704, 720), rt(688, 704),
                                         4.0, None, ALU.is_lt))
            rt_score_read[t % 2] = dv(lambda v: v.tensor_mul(
                rt(672, 688), rt(704, 720), rt(16, 32)))
            dv(lambda v: v.reduce_sum(st_sb[:, 60:61], rt(672, 688), axis=AXX))
            dv(lambda v: v.tensor_scalar(st_sb[:, 61:62], st_sb[:, 60:61],
                                         1e-20, None, ALU.add))
            dv(lambda v: v.reciprocal(st_sb[:, 62:63], st_sb[:, 61:62]))
            cb_cnt = dv(lambda v, t=t: v.tensor_scalar(
                comb_sb[:, NE * t: NE * t + 16], rt(672, 688),
                st_sb[:, 62:63], 1.25, ALU.mult, ALU.mult))
            comb_w[t] = cb_cnt
            return ev_cnt

        def emit_ffn(tiles):
            t0, t1 = tiles[0], tiles[-1] + 1
            c0, c1 = 128 * t0, 128 * t1
            slices = []
            s = c0
            while s < c1:
                slices.append((s, min(s + 512, c1)))
                s += 512

            st_cnt = emit_stats_batch(tiles)
            xn_war = p.count("spe")
            xn_tile_done = {}
            pend = []
            for t in tiles:
                xn_tile_done[t] = emit_xn_tile(
                    t, lambda c, t: xnc(c, 128 * t, 128 * (t + 1)), st_cnt,
                    war_pe=xn_war)
                pend.append((t, xn_tile_done[t]))

            g3, u3, d3 = wsrc(0)
            load_matrix(0, g3, slot_free[0])
            load_matrix(1, u3, slot_free[1])
            load_matrix(2, d3, slot_free[2])

            down_done_prev[0] = max(down_done_prev[0], ring_free_pe[0])
            for e in range(NE):
                gate_ready = slot_half_ready[0][:]
                up_ready = slot_half_ready[1][:]
                down_ready = slot_half_ready[2][1]
                hu_ready = {}
                last_gate_mm = None
                last_gu_mm = None
                for si, (s0, s1) in enumerate(slices):
                    n = s1 - s0
                    tdone = xn_tile_done[(s1 - 1) // 128]
                    p.wait("pe", "sx", tdone[0])
                    p.wait("pe", "sxa", tdone[1])
                    for m in range(8):
                        bkG = "A" if m % 2 == 0 else "C"
                        bkU = "B" if m % 2 == 0 else "D"
                        bs, bv = bank_free[bkG]
                        p.wait("pe", bs, bv)
                        for k in range(8):
                            p.wait("pe", "sw", gate_ready[k // 4])
                            fn = lambda t_, k=k, m=m, s0=s0, s1=s1, bkG=bkG: t_.matmul(
                                PS[bkG][:, 0:s1 - s0], slot_chunk(0, k, 128 * m, 128 * (m + 1)),
                                xnc(k, s0, s1), start=(k == 0), stop=(k == 7))
                            if k == 7:
                                gate_cnt = p.inc("pe", "spe", fn)
                            else:
                                p.raw("pe", ("r", fn))
                        last_gate_mm = gate_cnt
                        bs, bv = bank_free[bkU]
                        p.wait("pe", bs, bv)
                        for k in range(8):
                            p.wait("pe", "sw", up_ready[k // 4])
                            fn = lambda t_, k=k, m=m, s0=s0, s1=s1, bkU=bkU: t_.matmul(
                                PS[bkU][:, 0:s1 - s0], slot_chunk(1, k, 128 * m, 128 * (m + 1)),
                                xnc(k, s0, s1), start=(k == 0), stop=(k == 7))
                            if k == 7:
                                up_cnt = p.inc("pe", "spe", fn)
                            else:
                                p.raw("pe", ("r", fn))
                        last_gu_mm = up_cnt
                        if pend:
                            emit_routing(*pend.pop(0))
                        par = m % 2
                        p.wait("ac", "spe", gate_cnt)
                        p.wait("ac", "sdv", gs_free_ac[par])
                        silu_cnt = p.inc("ac", "sac", lambda s_, par=par, n=n, bkG=bkG:
                                         s_.activation(gs_sb[:, 512 * par: 512 * par + n],
                                                       PS[bkG][:, 0:n], AF.Silu))
                        gs_w_ac[par] = silu_cnt
                        bank_free[bkG] = ("sac", silu_cnt)
                        p.wait("dv", "spe", up_cnt)
                        p.wait("dv", "sac", silu_cnt)
                        p.wait("dv", "spe", down_done_prev[0])
                        hu_cnt = p.inc("dv", "sdv", lambda v, m=m, par=par, n=n, bkU=bkU:
                                       v.tensor_tensor(hu_sb[:, 512 * m: 512 * m + n],
                                                       gs_sb[:, 512 * par: 512 * par + n],
                                                       PS[bkU][:, 0:n], ALU.mult))
                        gs_free_ac[par] = hu_cnt
                        bank_free[bkU] = ("sdv", hu_cnt)
                        hu_ready[m] = hu_cnt
                    ntile = (n + 127) // 128
                    for mt in range(ntile):
                        tglob = (s0 // 128) + mt
                        bk0 = "E" if mt % 2 == 0 else "G"
                        bk1 = "F" if mt % 2 == 0 else "H"
                        p.wait("pe", "sw", down_ready)
                        bs, bv = bank_free[bk0]
                        p.wait("pe", bs, bv)
                        bs, bv = bank_free[bk1]
                        p.wait("pe", bs, bv)
                        for nn in range(2):
                            bk = bk0 if nn == 0 else bk1
                            for k in range(8):
                                p.wait("pe", "sdv", hu_ready[k])
                                fn = lambda t_, k=k, nn=nn, mt=mt, bk=bk: t_.matmul(
                                    PS[bk][:, 0:512],
                                    hu_sb[:, 512 * k + 128 * mt: 512 * k + 128 * (mt + 1)],
                                    slot_chunk(2, k, 512 * nn, 512 * (nn + 1)),
                                    start=(k == 0), stop=(k == 7))
                                if nn == 1 and k == 7:
                                    dn_cnt = p.inc("pe", "spe", fn)
                                else:
                                    p.raw("pe", ("r", fn))
                        down_done_prev[0] = dn_cnt
                        p.wait("dv", "spe", dn_cnt)
                        xs, xv = x_w[tglob]
                        p.wait("dv", xs, xv)
                        p.wait("dv", "sac", x_r_ac[tglob])
                        assert comb_w[tglob] > 0, f"routing for tile {tglob} not emitted yet"
                        p.wait("dv", "sdv", comb_w[tglob])
                        for nn in range(2):
                            bk = bk0 if nn == 0 else bk1
                            upd_cnt = p.inc("dv", "sdv", lambda v, bk=bk, nn=nn, tglob=tglob, e=e:
                                            v.scalar_tensor_tensor(
                                                x_sb[:, D * tglob + 512 * nn: D * tglob + 512 * (nn + 1)],
                                                PS[bk][:, 0:512],
                                                comb_sb[:, NE * tglob + e: NE * tglob + e + 1],
                                                x_sb[:, D * tglob + 512 * nn: D * tglob + 512 * (nn + 1)],
                                                ALU.mult, ALU.add))
                        x_w[tglob] = ("sdv", upd_cnt)
                        bank_free[bk0] = ("sdv", upd_cnt)
                        bank_free[bk1] = ("sdv", upd_cnt)
                slot_free[0] = last_gate_mm
                slot_free[1] = last_gu_mm
                slot_free[2] = down_done_prev[0]
                if e + 1 < NE:
                    g3, u3, d3 = wsrc(e + 1)
                    load_matrix(0, g3, slot_free[0])
                    load_matrix(1, u3, slot_free[1])
                    load_matrix(2, d3, slot_free[2])

        xn_tm_free = [(0, 0)]
        down_done_prev = [0]
        ln_ts_last = [0]
        ring_free_pe = [0]
        rt_score_read = [0, 0]
        comb_w = {t: 0 for t in range(NT)}

        def emit_attn():
            p.wait("gp", "spe", slot_free[0])
            for h in range(2):
                wq_cnt = p.inc("gp", "sw", lambda g, h=h: g.dma_start(
                    out=xn_sb[:, 4096 * h: 4096 * (h + 1)],
                    in_=wq_w[:, 4096 * h: 4096 * (h + 1)]), 16)
            load_matrix(0, wk_w, slot_free[0])
            wk_cnt = slot_ready[0]
            load_matrix(1, wv_w, slot_free[1])
            wv_cnt = slot_ready[1]
            load_matrix(2, wo_w, slot_free[2])
            wo_cnt = slot_ready[2]

            ring_lastread = {0: 0, 1: 0, 2: 0, 3: 0}
            ring_ready = {0: 0, 1: 0, 2: 0, 3: 0}
            ring_war_dv = p.count("sdv")
            xnr_lastread = [0, 0]
            prT_free = {0: 0, 1: 0, 2: 0}
            pr32_free = {0: 0, 1: 0, 2: 0}
            sc_free = {0: 0, 1: 0, 2: 0}
            attn_pe_start = [p.count("spe")]
            o_free = {cc: 0 for cc in range(8)}
            o_ev = {cc: 0 for cc in range(8)}
            q_ready_t = {}
            prep_done = {}
            head_nrm = {}
            head_exp = {}
            head_pt = {}
            pend_wo = [None]
            wlast = {"k": 0, "v": 0, "o": 0}

            def xnr(r2, cc):
                return xn_sb[:, 8192 + 1024 * r2 + 128 * cc:
                             8192 + 1024 * r2 + 128 * (cc + 1)]

            st_cnt = emit_stats_batch(list(range(NT)))

            def prep(tau):
                prep_done[tau] = emit_xn_tile(
                    tau, lambda cc, tt, r2=tau % 2: xnr(r2, cc), st_cnt,
                    queues=("sy",), war_pe=xnr_lastread[tau % 2])

            def mk_projK(tau, cc):
                def emit():
                    r, r2 = tau % 4, tau % 2
                    bk = "A" if cc % 2 == 0 else "B"
                    bs, bv = bank_free[bk]
                    p.wait("pe", bs, bv)
                    p.wait("pe", "sx", prep_done[tau][0])
                    p.wait("pe", "sw", wk_cnt)
                    for k in range(8):
                        fn = lambda t_, k=k, cc=cc, r2=r2, bk=bk: t_.matmul(
                            PS[bk][:, 0:128], slot_chunk(0, k, 128 * cc, 128 * (cc + 1)),
                            xnr(r2, k), start=(k == 0), stop=(k == 7))
                        if k == 7:
                            kp_cnt = p.inc("pe", "spe", fn)
                        else:
                            p.raw("pe", ("r", fn))
                    xnr_lastread[r2] = max(xnr_lastread[r2], kp_cnt)
                    wlast["k"] = kp_cnt
                    p.wait("ac", "spe", kp_cnt)
                    p.wait("ac", "spe", ring_lastread[r])
                    p.wait("ac", "sdv", ring_war_dv)
                    kv_cnt = p.inc("ac", "sac", lambda s, cc=cc, r=r, bk=bk: s.add(
                        kr(cc, r), PS[bk][:, 0:128], bcol[:, 8 + cc: 9 + cc]))
                    bank_free[bk] = ("sac", kv_cnt)
                    ring_ready[r] = max(ring_ready[r], kv_cnt)
                return emit

            def mk_projV(tau, nn):
                def emit():
                    r, r2 = tau % 4, tau % 2
                    bk = "C"
                    bs, bv = bank_free[bk]
                    p.wait("pe", bs, bv)
                    p.wait("pe", "sx", prep_done[tau][0])
                    p.wait("pe", "sw", wv_cnt)
                    for k in range(8):
                        fn = lambda t_, k=k, nn=nn, r2=r2, bk=bk: t_.matmul(
                            PS[bk][:, 0:512], xnr(r2, k),
                            slot_chunk(1, k, 512 * nn, 512 * (nn + 1)),
                            start=(k == 0), stop=(k == 7))
                        if k == 7:
                            vp_cnt = p.inc("pe", "spe", fn)
                        else:
                            p.raw("pe", ("r", fn))
                    xnr_lastread[r2] = max(xnr_lastread[r2], vp_cnt)
                    wlast["v"] = vp_cnt
                    p.wait("ac", "spe", vp_cnt)
                    p.wait("ac", "spe", ring_lastread[r])
                    p.wait("ac", "sdv", ring_war_dv)
                    vv_cnt = p.inc("ac", "sac", lambda s, nn=nn, r=r, bk=bk: s.copy(
                        vr(r)[:, 512 * nn: 512 * (nn + 1)], PS[bk][:, 0:512]))
                    bank_free[bk] = ("sac", vv_cnt)
                    ring_ready[r] = max(ring_ready[r], vv_cnt)
                return emit

            def projQ(tau):
                r2 = tau % 2
                qev = []
                for cc in range(8):
                    bk = "A" if cc % 2 == 0 else "B"
                    bs, bv = bank_free[bk]
                    p.wait("pe", bs, bv)
                    p.wait("pe", "sx", prep_done[tau][0])
                    p.wait("pe", "sw", wq_cnt)
                    for k in range(8):
                        fn = lambda t_, k=k, cc=cc, r2=r2, bk=bk: t_.matmul(
                            PS[bk][:, 0:128],
                            xn_sb[:, 1024 * k + 128 * cc: 1024 * k + 128 * (cc + 1)],
                            xnr(r2, k), start=(k == 0), stop=(k == 7))
                        if k == 7:
                            qp_cnt = p.inc("pe", "spe", fn)
                        else:
                            p.raw("pe", ("r", fn))
                    xnr_lastread[r2] = max(xnr_lastread[r2], qp_cnt)
                    p.wait("ac", "spe", qp_cnt)
                    p.wait("ac", "spe", q_lastread[0])
                    qv_cnt = p.inc("ac", "sac", lambda s, cc=cc, bk=bk: s.add(
                        q_sb[:, 128 * cc: 128 * (cc + 1)], PS[bk][:, 0:128],
                        bcol[:, cc: cc + 1]))
                    bank_free[bk] = ("sac", qv_cnt)
                    qev.append(qv_cnt)
                q_ready_t[tau] = max(qev)

            for tau in range(3):
                prep(tau)
                for it in [mk_projK(tau, cc) for cc in range(8)] +                           [mk_projV(tau, nn) for nn in range(2)]:
                    it()
            projQ(2)

            for t in range(2, NT):
                if t + 1 < NT:
                    prep(t + 1)
                    feeds = [mk_projK(t + 1, cc) for cc in range(8)] +                             [mk_projV(t + 1, nn) for nn in range(2)]
                else:
                    feeds = []
                v_mask = min(t - 2, 2)
                slots = [(t - 2) % 4, (t - 1) % 4, t % 4]
                q_ready = q_ready_t[t]

                def head_front(h):
                    rho = h % 3
                    cc = h // 2
                    pe_off = (h % 2) * 64
                    bkS = BKS[rho]
                    bs, bv = bank_free[bkS]
                    p.wait("pe", bs, bv)
                    p.wait("pe", "sac", q_ready)
                    for j in range(3):
                        p.wait("pe", "sac", ring_ready[slots[j]])
                    for j in range(3):
                        fn = lambda t_, j=j, cc=cc, pe_off=pe_off, slots=slots, bkS=bkS: \
                            t_.matmul(
                                PS[bkS][:, 128 * j: 128 * (j + 1)],
                                q_sb[pe_off: pe_off + 64, 128 * cc: 128 * (cc + 1)],
                                kr(cc, slots[j])[pe_off: pe_off + 64, :],
                                start=True, stop=True)
                        if j == 2:
                            sc_cnt = p.inc("pe", "spe", fn)
                        else:
                            p.raw("pe", ("r", fn))
                    q_lastread[0] = sc_cnt
                    p.wait("dv", "spe", sc_cnt)
                    p.wait("dv", "sac", sc_free[rho])
                    p.wait("dv", "sac", x_r_ac[t])
                    p.wait("dv", "sx", init_cnt)
                    if rho == 2:
                        p.wait("dv", "spe", attn_pe_start[0])
                    scm_cnt = p.inc("dv", "sdv", lambda v, rho=rho, v_mask=v_mask, bkS=bkS:
                                    v.scalar_tensor_tensor(
                                        sc_buf(rho), PS[bkS][:, 0:384], 0.125,
                                        mask_sb[:, 384 * v_mask: 384 * (v_mask + 1)],
                                        ALU.mult, ALU.add))
                    p.wait("dv", "sdv", scm_cnt)
                    nm_cnt = p.inc("dv", "sdv", lambda v, rho=rho: v.reduce_max(
                        st_sb[:, NMX[rho]: NMX[rho] + 1], sc_buf(rho),
                        axis=AXX, negate=True))
                    if rho < 2:
                        gs_free_ac[rho] = nm_cnt
                    p.wait("ac", "sdv", nm_cnt)
                    p.wait("ac", "spe", pr32_free[rho])
                    exp_cnt = p.inc("ac", "sac", lambda s, rho=rho: s.activation(
                        pr_buf(rho), sc_buf(rho), AF.Exp,
                        bias=st_sb[:, NMX[rho]: NMX[rho] + 1],
                        accum_out=st_sb[:, DEN[rho]: DEN[rho] + 1]))
                    sc_free[rho] = exp_cnt
                    head_exp[h] = exp_cnt

                def head_front_b(h):
                    rho = h % 3
                    exp_cnt = head_exp[h]
                    p.wait("dv", "sac", exp_cnt)
                    rc_cnt = p.inc("dv", "sdv", lambda v, rho=rho: v.reciprocal(
                        st_sb[:, RDE[rho]: RDE[rho] + 1], st_sb[:, DEN[rho]: DEN[rho] + 1]))
                    p.wait("dv", "sdv", rc_cnt)
                    nrm_cnt = p.inc("dv", "sdv", lambda v, rho=rho: v.tensor_scalar(
                        pr_buf(rho), pr_buf(rho),
                        st_sb[:, RDE[rho]: RDE[rho] + 1], None, ALU.mult))
                    head_nrm[h] = nrm_cnt

                def head_tail(h):
                    rho = h % 3
                    par = h % 2
                    cc = h // 2
                    pe_off = (h % 2) * 64
                    bkS = BKS[rho]
                    p.wait("pe", "sdv", head_nrm[h])
                    for j in range(3):
                        fn = lambda t_, j=j, rho=rho, bkS=bkS: t_.transpose(
                            PS[bkS][:, 128 * j: 128 * (j + 1)],
                            pr_buf(rho)[:, 128 * j: 128 * (j + 1)],
                            ident[:])
                        if j == 2:
                            trp_cnt = p.inc("pe", "spe", fn)
                        else:
                            p.raw("pe", ("r", fn))
                    pr32_free[rho] = trp_cnt
                    p.wait("dv", "spe", trp_cnt)
                    p.wait("dv", "spe", prT_free[rho])
                    pt_cnt = p.inc("dv", "sdv", lambda v, rho=rho, bkS=bkS: v.tensor_copy(
                        pt_buf(rho), PS[bkS][:, 0:384]))
                    bank_free[bkS] = ("sdv", pt_cnt)
                    head_pt[h] = pt_cnt

                def head_tail_b(h):
                    rho = h % 3
                    par = h % 2
                    cc = h // 2
                    pe_off = (h % 2) * 64
                    pt_cnt = head_pt[h]
                    bkO = "G" if cc % 2 == 0 else "H"
                    if par == 0:
                        bs, bv = bank_free[bkO]
                        p.wait("pe", bs, bv)
                    p.wait("pe", "sdv", pt_cnt)
                    for j in range(3):
                        fn = lambda t_, j=j, h=h, rho=rho, pe_off=pe_off, slots=slots, bkO=bkO: \
                            t_.matmul(PS[bkO][pe_off: pe_off + 64, 0:128],
                                      vr(slots[j])[:, 64 * h: 64 * (h + 1)],
                                      pt_buf(rho)[:, 128 * j: 128 * (j + 1)],
                                      start=(j == 0), stop=(j == 2))
                        if j == 2:
                            av_cnt = p.inc("pe", "spe", fn)
                        else:
                            p.raw("pe", ("r", fn))
                    prT_free[rho] = av_cnt
                    for j in range(3):
                        ring_lastread[slots[j]] = max(ring_lastread[slots[j]], av_cnt)
                    if par == 1:
                        p.wait("ac", "spe", av_cnt)
                        p.wait("ac", "spe", o_free[cc])
                        oc_cnt = p.inc("ac", "sac", lambda s, cc=cc, bkO=bkO: s.add(
                            o_sb[:, 128 * cc: 128 * (cc + 1)], PS[bkO][:, 0:128],
                            bcol[:, 16 + cc: 17 + cc]))
                        bank_free[bkO] = ("sac", oc_cnt)
                        o_ev[cc] = oc_cnt

                for h in range(NH):
                    head_front(h)
                    if h == 1 and pend_wo[0] is not None:
                        pend_wo[0](0)
                    if h >= 1:
                        head_tail(h - 1)
                    head_front_b(h)
                    if h == 1 and pend_wo[0] is not None:
                        pend_wo[0](1)
                        pend_wo[0] = None
                    if h >= 1:
                        head_tail_b(h - 1)
                    if 6 <= h < 6 + len(feeds):
                        feeds[h - 6]()
                head_tail(NH - 1)
                head_tail_b(NH - 1)
                for it in feeds[10:]:
                    it()
                if t + 1 < NT:
                    projQ(t + 1)
                def mk_wo(t, o_ev_snap):
                    def emit(nn):
                        p.wait("pe", "sw", wo_cnt)
                        for cc in range(8):
                            p.wait("pe", "sac", o_ev_snap[cc])
                        bk = "C"
                        bs, bv = bank_free[bk]
                        p.wait("pe", bs, bv)
                        for k in range(8):
                            fn = lambda t_, k=k, nn=nn, bk=bk: t_.matmul(
                                PS[bk][:, 0:512], o_sb[:, 128 * k: 128 * (k + 1)],
                                slot_chunk(2, k, 512 * nn, 512 * (nn + 1)),
                                start=(k == 0), stop=False)
                            p.raw("pe", ("r", fn))
                        p.wait("pe", "sw", 32)
                        fn = lambda t_, nn=nn, bk=bk: t_.matmul(
                            PS[bk][:, 0:512], ones_bf[0:1, 0:128],
                            bo_sb[0:1, 512 * nn: 512 * (nn + 1)], start=False, stop=True)
                        wo_mm = p.inc("pe", "spe", fn)
                        wlast["o"] = wo_mm
                        p.wait("dv", "spe", wo_mm)
                        xs, xv = x_w[t]
                        p.wait("dv", xs, xv)
                        p.wait("dv", "sac", x_r_ac[t])
                        upd = p.inc("dv", "sdv", lambda v, t=t, nn=nn, bk=bk: v.tensor_add(
                            x_sb[:, D * t + 512 * nn: D * t + 512 * (nn + 1)],
                            x_sb[:, D * t + 512 * nn: D * t + 512 * (nn + 1)],
                            PS[bk][:, 0:512]))
                        bank_free[bk] = ("sdv", upd)
                        x_w[t] = ("sdv", upd)
                        if nn == 1:
                            for cc in range(8):
                                o_free[cc] = wo_mm
                    return emit
                pend_wo[0] = mk_wo(t, dict(o_ev))

            if pend_wo[0] is not None:
                pend_wo[0](0)
                pend_wo[0](1)
                pend_wo[0] = None
            slot_free[0] = wlast["k"]
            slot_free[1] = wlast["v"]
            slot_free[2] = wlast["o"]
            ring_free_pe[0] = p.count("spe")

        q_lastread = [0]
        ln_last_rstd = [0]

        if phases & 1:
            emit_ffn(list(range(NT)))
        if phases & 2:
            emit_attn()
        if phases & 4:
            emit_ffn(list(range(2, NT)))

        for i in range(4):
            tiles = range(2 + 4 * i, 2 + 4 * (i + 1))
            for t in tiles:
                xs, xv = x_w[t]
                p.wait("sy", xs, xv)
            p.inc("sy", "sst", lambda sy, i=i: sy.dma_start(
                out=out.rearrange("(t p) d -> t p d", p=128)[4 * i: 4 * (i + 1)].transpose([1, 0, 2]),
                in_=x_sb[:, D * (2 + 4 * i): D * (2 + 4 * (i + 1))].rearrange(
                    "p (t d) -> p t d", t=4)), 16)
        p.wait("sy", "sst", p.count("sst"))

        def run(eng_obj, key):
            for item in p.ops[key]:
                if callable(item):
                    item(eng_obj)
                elif item[0] == "w":
                    eng_obj.wait_ge(SEM[item[1]], item[2])
                elif item[0] == "i":
                    item[3](eng_obj).then_inc(SEM[item[1]], item[2])
                elif item[0] == "r":
                    item[1](eng_obj)

        @block.tensor
        def _(t):
            run(t, "pe")

        @block.vector
        def _(v):
            run(v, "dv")

        @block.scalar
        def _(s):
            run(s, "ac")

        @block.gpsimd
        def _(g):
            run(g, "gp")

        @block.sync
        def _(sy):
            run(sy, "sy")

    return nc



_NC_CACHE = {}

def _get_program():
    if "nc" not in _NC_CACHE:
        _NC_CACHE["nc"] = build_program()
    return _NC_CACHE["nc"]


def _build_masks(is_start):
    i = np.arange(128)[:, None]
    m = np.empty((3, 128, 384), np.float32)
    c = np.arange(384)[None, :]
    band = (c > i) & (c <= i + 256)
    for v in range(3):
        allow = band.copy()
        if is_start and v < 2:
            allow &= c >= (256 - 128 * v)
        m[v] = np.where(allow, 0.0, NEG)
    return m


def _build_cmask():
    cm = np.zeros((128, 544), np.float32)
    i4 = np.arange(4)
    L4 = (i4[None, :] < i4[:, None]).astype(np.float32).reshape(-1)
    U4 = (i4[None, :] > i4[:, None]).astype(np.float32).reshape(-1)
    i16 = np.arange(16)
    L16 = (i16[None, :] < i16[:, None]).astype(np.float32).reshape(-1)
    U16 = (i16[None, :] > i16[:, None]).astype(np.float32).reshape(-1)
    cm[:, 0:16] = L4
    cm[:, 16:32] = U4
    cm[:, 32:288] = L16
    cm[:, 288:544] = U16
    return cm


def _wt(w):
    w = np.asarray(w, np.float32)
    return np.ascontiguousarray(w.reshape(8, 128, -1).transpose(1, 0, 2).reshape(128, -1))


def _wt3(w):
    w = np.asarray(w, np.float32)
    return np.ascontiguousarray(w.reshape(16, 8, 128, 1024).transpose(0, 2, 1, 3).reshape(16, 128, 8192))


def kernel(L0_attn_x, L0_attn_wq, L0_attn_bq, L0_attn_wk, L0_attn_bk,
           L0_attn_wv, L0_attn_bv, L0_attn_wo, L0_attn_bo,
           L0_ffn_router_w, L0_ffn_bias, L0_ffn_gate_w, L0_ffn_up_w,
           L0_ffn_down_w, L0_ffn_sh_gate_w, L0_ffn_sh_up_w, L0_ffn_sh_down_w):
    x = np.ascontiguousarray(np.asarray(L0_attn_x, np.float32).reshape(16384, D))
    bq = np.asarray(L0_attn_bq, np.float32)
    bk = np.asarray(L0_attn_bk, np.float32)
    bv = np.asarray(L0_attn_bv, np.float32)
    bqkv = np.concatenate([bq.reshape(8, 128).T, bk.reshape(8, 128).T,
                           bv.reshape(8, 128).T], axis=1).astype(np.float32)
    cmask = _build_cmask()
    rbias_b = np.broadcast_to(np.asarray(L0_ffn_bias, np.float32)[None, :],
                              (128, 16)).copy()
    identw = np.eye(128, dtype=np.float32)
    onesw = np.ones((1, 512), np.float32)
    shared = {
        "cmask": cmask, "identw": identw, "bqkv": bqkv, "rbias_b": rbias_b,
        "onesw": onesw, "rw": np.asarray(L0_ffn_router_w, np.float32),
        "bo_w": np.asarray(L0_attn_bo, np.float32).reshape(1, D),
        "wq_w": _wt(L0_attn_wq), "wk_w": _wt(L0_attn_wk),
        "wv_w": _wt(L0_attn_wv), "wo_w": _wt(L0_attn_wo),
        "gate_w": _wt3(L0_ffn_gate_w), "up_w": _wt3(L0_ffn_up_w),
        "down_w": _wt3(L0_ffn_down_w),
        "shg_w": np.stack([_wt(np.asarray(L0_ffn_sh_gate_w)[:, 1024 * h: 1024 * (h + 1)])
                           for h in range(2)]),
        "shu_w": np.stack([_wt(np.asarray(L0_ffn_sh_up_w)[:, 1024 * h: 1024 * (h + 1)])
                           for h in range(2)]),
        "shd_w": np.stack([_wt(np.asarray(L0_ffn_sh_down_w)[1024 * h: 1024 * (h + 1)])
                           for h in range(2)]),
    }
    in_maps = []
    for c in range(8):
        start = (c % 4 == 0)
        r0 = c * 2048
        if start:
            halo = np.zeros((HALO, D), np.float32)
        else:
            halo = x[r0 - HALO: r0]
        xin = np.concatenate([halo, x[r0: r0 + 2048]], 0)
        m = dict(shared)
        m["xin"] = np.ascontiguousarray(xin)
        m["masks"] = _build_masks(start)
        in_maps.append(m)
    nc = _get_program()
    res = run_bass_kernel_spmd(nc, in_maps, core_ids=list(range(8)))
    outp = np.concatenate([np.asarray(res.results[c]["out"], np.float32)
                           for c in range(8)], axis=0)
    return outp.reshape(2, 8192, D)


# version 14
# speedup vs baseline: 1.1790x; 1.1790x over previous
import numpy as np
import concourse.bass as bass
import concourse.mybir as mybir
from concourse.bass_utils import run_bass_kernel_spmd

F32 = mybir.dt.float32
BF16 = mybir.dt.bfloat16
AF = mybir.ActivationFunctionType
ALU = mybir.AluOpType
AXX = mybir.AxisListType.X

D = 1024
NT = 18
NT_OWN = 16
TOK = NT * 128
HALO = 256
NH = 16
HD = 64
NE = 18
NEG = -1.0e30


class P:
    def __init__(self):
        self.ops = {"pe": [], "dv": [], "ac": [], "gp": [], "sy": []}
        self.cnt = {}
        self.waited = {e: {} for e in self.ops}

    def count(self, sem):
        return self.cnt.get(sem, 0)

    def raw(self, eng, fn):
        self.ops[eng].append(fn)

    def wait(self, eng, sem, val):
        if val is None or val <= 0:
            return
        assert val <= self.count(sem), f"wait {sem}>={val} > emitted {self.count(sem)}"
        if val > self.waited[eng].get(sem, 0):
            self.waited[eng][sem] = val
            self.ops[eng].append(("w", sem, val))

    def inc(self, eng, sem, fn, n=1):
        self.cnt[sem] = self.count(sem) + n
        self.ops[eng].append(("i", sem, n, fn))
        return self.cnt[sem]


def build_program(phases=7):
    nc = bass.Bass("TRN2", target_bir_lowering=False)

    xin = nc.dram_tensor("xin", [TOK, D], F32, kind="ExternalInput").ap()
    masks = nc.dram_tensor("masks", [3, 128, 384], F32, kind="ExternalInput").ap()
    cmask = nc.dram_tensor("cmask", [128, 544], F32, kind="ExternalInput").ap()
    identw = nc.dram_tensor("identw", [128, 128], F32, kind="ExternalInput").ap()
    bqkv = nc.dram_tensor("bqkv", [128, 24], F32, kind="ExternalInput").ap()
    rbias_b = nc.dram_tensor("rbias_b", [128, 16], F32, kind="ExternalInput").ap()
    onesw = nc.dram_tensor("onesw", [1, 512], F32, kind="ExternalInput").ap()
    rw = nc.dram_tensor("rw", [16, D], F32, kind="ExternalInput").ap()
    bo_w = nc.dram_tensor("bo_w", [1, D], F32, kind="ExternalInput").ap()
    wq_w = nc.dram_tensor("wq_w", [128, 8192], F32, kind="ExternalInput").ap()
    wk_w = nc.dram_tensor("wk_w", [128, 8192], F32, kind="ExternalInput").ap()
    wv_w = nc.dram_tensor("wv_w", [128, 8192], F32, kind="ExternalInput").ap()
    wo_w = nc.dram_tensor("wo_w", [128, 8192], F32, kind="ExternalInput").ap()
    gate_w = nc.dram_tensor("gate_w", [16, 128, 8192], F32, kind="ExternalInput").ap()
    up_w = nc.dram_tensor("up_w", [16, 128, 8192], F32, kind="ExternalInput").ap()
    down_w = nc.dram_tensor("down_w", [16, 128, 8192], F32, kind="ExternalInput").ap()
    shg_w = nc.dram_tensor("shg_w", [2, 128, 8192], F32, kind="ExternalInput").ap()
    shu_w = nc.dram_tensor("shu_w", [2, 128, 8192], F32, kind="ExternalInput").ap()
    shd_w = nc.dram_tensor("shd_w", [2, 128, 8192], F32, kind="ExternalInput").ap()
    out = nc.dram_tensor("out", [NT_OWN * 128, D], F32, kind="ExternalOutput").ap()
    wbf = nc.dram_tensor("wbf", [58, 128, 8192], BF16, kind="Internal").ap()

    import contextlib
    with contextlib.ExitStack() as _ctx:
        x_sb = _ctx.enter_context(nc.sbuf_tensor("x_sb", [128, NT * D], F32))
        xn_sb = _ctx.enter_context(nc.sbuf_tensor("xn_sb", [128, 8 * TOK], BF16))
        wslot = _ctx.enter_context(nc.sbuf_tensor("wslot", [128, 3 * 8192], BF16))
        hu_sb = _ctx.enter_context(nc.sbuf_tensor("hu_sb", [128, 8 * 512], BF16))
        gs_sb = _ctx.enter_context(nc.sbuf_tensor("gs_sb", [128, 1024], F32))
        comb_sb = _ctx.enter_context(nc.sbuf_tensor("comb_sb", [128, NT * NE], F32))
        st_sb = _ctx.enter_context(nc.sbuf_tensor("st_sb", [128, 64], F32))
        arena = _ctx.enter_context(nc.sbuf_tensor("arena", [128, 8192], BF16))
        q_sb = _ctx.enter_context(nc.sbuf_tensor("q_sb", [128, 1024], BF16))
        pr32 = _ctx.enter_context(nc.sbuf_tensor("pr32", [128, 768], F32))
        prT = _ctx.enter_context(nc.sbuf_tensor("prT", [128, 768], BF16))
        o_sb = _ctx.enter_context(nc.sbuf_tensor("o_sb", [128, 1024], BF16))
        mask_sb = _ctx.enter_context(nc.sbuf_tensor("mask_sb", [128, 3 * 384], F32))
        cm_sb = _ctx.enter_context(nc.sbuf_tensor("cm_sb", [128, 544], F32))
        ident = _ctx.enter_context(nc.sbuf_tensor("ident", [128, 128], F32))
        ones_bf = _ctx.enter_context(nc.sbuf_tensor("ones_bf", [1, 512], BF16))
        bo_sb = _ctx.enter_context(nc.sbuf_tensor("bo_sb", [1, D], BF16))
        bcol = _ctx.enter_context(nc.sbuf_tensor("bcol", [128, 24], F32))
        rT_sb = _ctx.enter_context(nc.sbuf_tensor("rT_sb", [128, 128], BF16))
        rb_sb = _ctx.enter_context(nc.sbuf_tensor("rb_sb", [128, 16], F32))
        xn_tm = _ctx.enter_context(nc.sbuf_tensor("xn_tm", [128, 1024], BF16))
        pA = _ctx.enter_context(nc.psum_tensor("pA", [128, 512], F32))
        pB = _ctx.enter_context(nc.psum_tensor("pB", [128, 512], F32))
        pC = _ctx.enter_context(nc.psum_tensor("pC", [128, 512], F32))
        pD = _ctx.enter_context(nc.psum_tensor("pD", [128, 512], F32))
        pE = _ctx.enter_context(nc.psum_tensor("pE", [128, 512], F32))
        pF = _ctx.enter_context(nc.psum_tensor("pF", [128, 512], F32))
        pG = _ctx.enter_context(nc.psum_tensor("pG", [128, 512], F32))
        pH = _ctx.enter_context(nc.psum_tensor("pH", [128, 512], F32))
        sx = _ctx.enter_context(nc.semaphore("sx"))
        sw = _ctx.enter_context(nc.semaphore("sw"))
        spe = _ctx.enter_context(nc.semaphore("spe"))
        sdv = _ctx.enter_context(nc.semaphore("sdv"))
        sac = _ctx.enter_context(nc.semaphore("sac"))
        sst = _ctx.enter_context(nc.semaphore("sst"))
        sxa = _ctx.enter_context(nc.semaphore("sxa"))
        block = _ctx.enter_context(nc.Block())
        SEM = {"sx": sx, "sw": sw, "spe": spe, "sdv": sdv, "sac": sac, "sst": sst, "sxa": sxa}
        p = P()

        rt_sb = arena[:].bitcast(F32)

        def kr(c, s):
            if s < 4:
                return arena[:, 512 * c + 128 * s: 512 * c + 128 * (s + 1)]
            return hu_sb[:, 3072 + 128 * c: 3072 + 128 * (c + 1)]

        def vr(r):
            return arena[:, 4096 + 1024 * r: 4096 + 1024 * (r + 1)]

        def q_buf(tau):
            if tau % 2 == 0:
                return q_sb[:]
            return hu_sb[:, 2048: 3072]

        huF32 = hu_sb[:].bitcast(F32)

        def sc_buf(rho):
            if rho < 2:
                return gs_sb[:, 512 * rho: 512 * rho + 384]
            return huF32[:, 0:384]

        def pr_buf(rho):
            if rho < 2:
                return pr32[:, 384 * rho: 384 * rho + 384]
            return huF32[:, 384: 768]

        def pt_buf(rho):
            if rho < 2:
                return prT[:, 384 * rho: 384 * rho + 384]
            return hu_sb[:, 1536: 1920]

        NMX = {0: 54, 1: 55, 2: 50}
        DEN = {0: 56, 1: 57, 2: 51}
        RDE = {0: 58, 1: 59, 2: 52}
        BKS = {0: "E", 1: "F", 2: "D"}

        def xt(t):
            return x_sb[:, D * t: D * (t + 1)]

        def xnc(k, c0, c1):
            return xn_sb[:, TOK * k + c0: TOK * k + c1]

        def slot_chunk(s, k, f0, f1):
            return wslot[:, 8192 * s + 1024 * k + f0: 8192 * s + 1024 * k + f1]

        def wsrc(e):
            if e < 16:
                return gate_w[e], up_w[e], down_w[e]
            h = e - 16
            return shg_w[h], shu_w[h], shd_w[h]

        x_w = {t: ("sx", None) for t in range(NT)}
        x_r_ac = {t: 0 for t in range(NT)}
        bank_free = {b: ("spe", 0) for b in "ABCDEFGH"}
        PS = {"A": pA, "B": pB, "C": pC, "D": pD, "E": pE, "F": pF, "G": pG, "H": pH}

        xin3 = xin.rearrange("(t p) d -> t p d", p=128)
        for i in range(3):
            x_in_cnt = p.inc("sy", "sx", lambda sy, i=i: sy.dma_start(
                out=x_sb[:, D * 6 * i: D * 6 * (i + 1)].rearrange("p (t d) -> p t d", t=6),
                in_=xin3[6 * i: 6 * (i + 1)].transpose([1, 0, 2])), 16)
            for t in range(6 * i, 6 * (i + 1)):
                x_w[t] = ("sx", x_in_cnt)
        p.inc("sy", "sx", lambda sy: sy.dma_start(
            out=mask_sb[:].rearrange("p (v n) -> p v n", v=3),
            in_=masks.transpose([1, 0, 2])), 16)
        p.inc("sy", "sx", lambda sy: sy.dma_start(out=cm_sb[:], in_=cmask), 16)
        p.inc("sy", "sx", lambda sy: sy.dma_start(out=ident[:], in_=identw), 16)
        p.inc("sy", "sx", lambda sy: sy.dma_start(out=bcol[:], in_=bqkv), 16)
        p.inc("sy", "sx", lambda sy: sy.dma_start(out=rb_sb[:], in_=rbias_b), 16)
        rw_cnt = p.inc("sy", "sx", lambda sy: sy.dma_start(out=gs_sb[0:16, :], in_=rw), 16)
        init_cnt = p.count("sx")

        p.inc("gp", "sw", lambda g: g.dma_start(out=ones_bf[:], in_=onesw), 16)
        p.inc("gp", "sw", lambda g: g.dma_start(out=bo_sb[:], in_=bo_w), 16)

        p.wait("pe", "sx", rw_cnt)
        for c in range(8):
            rtr_cnt = p.inc("pe", "spe", lambda t, c=c: t.transpose(
                pF[:, 16 * c: 16 * (c + 1)], gs_sb[0:16, 128 * c: 128 * (c + 1)],
                ident[0:16, 0:16]))
        p.wait("ac", "spe", rtr_cnt)
        rT_cnt = p.inc("ac", "sac", lambda s: s.copy(rT_sb[:], pF[:, 0:128]))
        bank_free["F"] = ("sac", rT_cnt)
        comb3 = comb_sb[:].rearrange("p (t c) -> p t c", c=NE)
        comb_half_cnt = p.inc("dv", "sdv", lambda v: v.memset(comb3[:, :, 16:18], 0.5))

        slot_ready = [0, 0, 0]
        slot_half_ready = [[0, 0], [0, 0], [0, 0]]
        slot_free = [0, 0, 0]
        gs_free_ac = {0: 0, 1: 0}
        gs_w_ac = {0: 0, 1: 0}

        conv_done = {}
        conv_queue = []

        def drain_conversions(n):
            for _ in range(min(n, len(conv_queue))):
                j, s2 = conv_queue.pop(0)
                conv_done[j] = p.inc("gp", "sw", lambda g, j=j, s2=s2: g.dma_start(
                    out=wbf[j], in_=s2), 16)

        def load_matrix(slot, src2, war_pe, j=None):
            p.wait("gp", "spe", war_pe)
            if j is not None:
                p.wait("gp", "sw", conv_done[j])
                src2 = wbf[j]
            for h in range(2):
                cnt = p.inc("gp", "sw", lambda g, src2=src2, h=h, slot=slot: g.dma_start(
                    out=wslot[:, 8192 * slot + 4096 * h: 8192 * slot + 4096 * (h + 1)],
                    in_=src2[:, 4096 * h: 4096 * (h + 1)]), 16)
                slot_half_ready[slot][h] = cnt
            slot_ready[slot] = cnt

        def emit_stats_batch(tiles):
            lo, hi = tiles[0], tiles[-1] + 1
            p.wait("ac", "sdv", gs_free_ac[0])
            p.wait("ac", "sdv", gs_free_ac[1])
            p.wait("ac", "sdv", ln_ts_last[0])
            p.wait("dv", "sac", ln_last_rstd[0])
            for t in tiles:
                xs, xcnt = x_w[t]
                p.wait("dv", xs, xcnt)
                p.inc("dv", "sdv", lambda v, t=t: v.reduce_sum(
                    st_sb[:, t: t + 1], xt(t), axis=AXX))
                p.wait("ac", xs, xcnt)
                sq_cnt = p.inc("ac", "sac", lambda s, t=t: s.activation(
                    gs_sb[:, 0:1024], xt(t), AF.Square,
                    accum_out=st_sb[:, 18 + t: 19 + t]))
                x_r_ac[t] = sq_cnt
            rs_all = p.count("sdv")
            p.wait("dv", "sdv", rs_all)
            p.wait("dv", "sac", sq_cnt)
            V1 = st_sb[:, lo: hi]
            V2 = st_sb[:, 18 + lo: 18 + hi]
            MM = st_sb[:, 36 + lo: 36 + hi]
            p.inc("dv", "sdv", lambda v: v.tensor_scalar(V1, V1, 1.0 / D, None, ALU.mult))
            p.inc("dv", "sdv", lambda v: v.tensor_scalar(V2, V2, 1.0 / D, None, ALU.mult))
            p.wait("dv", "sdv", p.count("sdv"))
            p.inc("dv", "sdv", lambda v: v.tensor_mul(MM, V1, V1))
            p.wait("dv", "sdv", p.count("sdv"))
            var_cnt = p.inc("dv", "sdv", lambda v: v.scalar_tensor_tensor(
                V2, V2, 1e-5, MM, ALU.add, ALU.subtract))
            p.wait("ac", "sdv", var_cnt)
            p.inc("ac", "sac", lambda s: s.activation(V2, V2, AF.Ln))
            p.wait("ac", "sac", p.count("sac"))
            rstd_cnt = p.inc("ac", "sac", lambda s: s.activation(V2, V2, AF.Exp, scale=-0.5))
            ln_last_rstd[0] = rstd_cnt
            return rstd_cnt

        def emit_xn_tile(t, xn_dst, stats_sac, queues=("sy", "ac"),
                         war_pe=0):
            p.wait("dv", "sac", stats_sac)
            xs, xcnt = x_w[t]
            p.wait("dv", xs, xcnt)
            p.wait("dv", "sx", xn_tm_free[0][0])
            p.wait("dv", "sxa", xn_tm_free[0][1])
            xn_cnt = p.inc("dv", "sdv", lambda v, t=t: v.tensor_scalar(
                xn_tm[:], xt(t), st_sb[:, t: t + 1], st_sb[:, 18 + t: 19 + t],
                ALU.subtract, ALU.mult))
            ln_ts_last[0] = xn_cnt
            for qi, q in enumerate(queues):
                p.wait(q, "sdv", xn_cnt)
                p.wait(q, "spe", war_pe)
            qsem = {"sy": "sx", "ac": "sxa"}
            for c in range(8):
                q = queues[c % len(queues)]
                p.inc(q, qsem[q], lambda e, c=c, t=t: e.dma_start(
                    out=xn_dst(c, t), in_=xn_tm[:, 128 * c: 128 * (c + 1)],
                    transpose=True), 16)
            ev_cnt = (p.count("sx"), p.count("sxa"))
            xn_tm_free[0] = ev_cnt
            return ev_cnt

        def emit_routing(t, ev_cnt):
            ro = 768 * (t % 2)
            def rt(x, y):
                return rt_sb[:, ro + x: ro + y]
            bs, bv = bank_free["F"]
            p.wait("pe", bs, bv)
            p.wait("pe", "sac", rT_cnt)
            p.wait("pe", "sx", ev_cnt[0])
            p.wait("pe", "sxa", ev_cnt[1])
            for k in range(8):
                lg = lambda t_, k=k, t=t: t_.matmul(
                    pF[:, 0:16], xnc(k, 128 * t, 128 * (t + 1)),
                    rT_sb[:, 16 * k: 16 * (k + 1)], start=(k == 0), stop=(k == 7))
                if k == 7:
                    lg_cnt = p.inc("pe", "spe", lg)
                else:
                    p.raw("pe", ("r", lg))
            p.wait("ac", "spe", lg_cnt)
            p.wait("ac", "spe", ring_free_pe[0])
            p.wait("ac", "sdv", rt_score_read[t % 2])
            texp_cnt = p.inc("ac", "sac", lambda s: s.activation(
                rt(16, 32), pF[:, 0:16], AF.Exp, scale=-1.0))
            bank_free["F"] = ("sac", texp_cnt)
            def dv(fn):
                p.wait("dv", "sdv", p.count("sdv"))
                return p.inc("dv", "sdv", fn)
            p.wait("dv", "sac", texp_cnt)
            dv(lambda v: v.tensor_scalar(rt(0, 16), rt(16, 32), 1.0, None, ALU.add))
            dv(lambda v: v.reciprocal(rt(16, 32), rt(0, 16)))
            p.wait("dv", "sx", init_cnt)
            dv(lambda v: v.tensor_add(rt(32, 48), rt(16, 32), rb_sb[:]))
            sg = rt(32, 48).rearrange("p (g i) -> p g i", g=4)
            dv(lambda v: v.tensor_add(rt(48, 60).rearrange("p (g i) -> p g i", g=4),
                                      sg[:, :, 0:3], sg[:, :, 1:4]))
            dv(lambda v: v.tensor_add(rt(60, 68).rearrange("p (g i) -> p g i", g=4),
                                      sg[:, :, 0:2], sg[:, :, 2:4]))
            dv(lambda v: v.tensor_add(rt(68, 72).rearrange("p (g i) -> p g i", g=4),
                                      sg[:, :, 0:1], sg[:, :, 3:4]))
            dv(lambda v: v.reduce_max(rt(72, 76).unsqueeze(2),
                                      rt(48, 60).rearrange("p (g i) -> p g i", g=4), axis=AXX))
            dv(lambda v: v.reduce_max(rt(76, 80).unsqueeze(2),
                                      rt(60, 68).rearrange("p (g i) -> p g i", g=4), axis=AXX))
            dv(lambda v: v.tensor_max(rt(72, 76), rt(72, 76), rt(76, 80)))
            dv(lambda v: v.tensor_max(rt(80, 84), rt(72, 76), rt(68, 72)))
            g2 = rt(80, 84)
            A4 = g2.unsqueeze(1).broadcast_to([128, 4, 4])
            B4 = g2.unsqueeze(2).broadcast_to([128, 4, 4])
            X4 = rt(84, 100).rearrange("p (i j) -> p i j", i=4)
            Y4 = rt(100, 116).rearrange("p (i j) -> p i j", i=4)
            dv(lambda v: v.tensor_tensor(X4, A4, B4, ALU.is_ge))
            dv(lambda v: v.tensor_tensor(Y4, A4, B4, ALU.is_gt))
            L4 = cm_sb[:, 0:16].rearrange("p (i j) -> p i j", i=4)
            U4 = cm_sb[:, 16:32].rearrange("p (i j) -> p i j", i=4)
            p.wait("dv", "sx", init_cnt)
            dv(lambda v: v.tensor_tensor(X4, X4, L4, ALU.mult))
            dv(lambda v: v.tensor_tensor(Y4, Y4, U4, ALU.mult))
            dv(lambda v: v.tensor_tensor(X4, X4, Y4, ALU.add))
            dv(lambda v: v.reduce_sum(rt(116, 120).unsqueeze(2), X4, axis=AXX))
            dv(lambda v: v.tensor_scalar(rt(120, 124), rt(116, 120),
                                         2.0, None, ALU.is_lt))
            em = rt(120, 124).unsqueeze(2).broadcast_to([128, 4, 4])
            dv(lambda v: v.tensor_tensor(
                rt(124, 140).rearrange("p (g i) -> p g i", g=4), em, em, ALU.max))
            dv(lambda v: v.tensor_scalar(rt(140, 156), rt(32, 48),
                                         1.0, None, ALU.add))
            dv(lambda v: v.tensor_mul(rt(140, 156), rt(140, 156),
                                      rt(124, 140)))
            dv(lambda v: v.tensor_scalar(rt(140, 156), rt(140, 156),
                                         1.0, None, ALU.subtract))
            mk = rt(140, 156)
            A16 = mk.unsqueeze(1).broadcast_to([128, 16, 16])
            B16 = mk.unsqueeze(2).broadcast_to([128, 16, 16])
            X16 = rt(160, 416).rearrange("p (i j) -> p i j", i=16)
            Y16 = rt(416, 672).rearrange("p (i j) -> p i j", i=16)
            dv(lambda v: v.tensor_tensor(X16, A16, B16, ALU.is_ge))
            dv(lambda v: v.tensor_tensor(Y16, A16, B16, ALU.is_gt))
            L16 = cm_sb[:, 32:288].rearrange("p (i j) -> p i j", i=16)
            U16 = cm_sb[:, 288:544].rearrange("p (i j) -> p i j", i=16)
            dv(lambda v: v.tensor_tensor(X16, X16, L16, ALU.mult))
            dv(lambda v: v.tensor_tensor(Y16, Y16, U16, ALU.mult))
            dv(lambda v: v.tensor_tensor(X16, X16, Y16, ALU.add))
            dv(lambda v: v.reduce_sum(rt(688, 704).unsqueeze(2), X16, axis=AXX))
            dv(lambda v: v.tensor_scalar(rt(704, 720), rt(688, 704),
                                         4.0, None, ALU.is_lt))
            rt_score_read[t % 2] = dv(lambda v: v.tensor_mul(
                rt(672, 688), rt(704, 720), rt(16, 32)))
            dv(lambda v: v.reduce_sum(st_sb[:, 60:61], rt(672, 688), axis=AXX))
            dv(lambda v: v.tensor_scalar(st_sb[:, 61:62], st_sb[:, 60:61],
                                         1e-20, None, ALU.add))
            dv(lambda v: v.reciprocal(st_sb[:, 62:63], st_sb[:, 61:62]))
            cb_cnt = dv(lambda v, t=t: v.tensor_scalar(
                comb_sb[:, NE * t: NE * t + 16], rt(672, 688),
                st_sb[:, 62:63], 1.25, ALU.mult, ALU.mult))
            comb_w[t] = cb_cnt
            return ev_cnt

        def emit_ffn(tiles):
            t0, t1 = tiles[0], tiles[-1] + 1
            c0, c1 = 128 * t0, 128 * t1
            slices = []
            s = c0
            while s < c1:
                slices.append((s, min(s + 512, c1)))
                s += 512

            st_cnt = emit_stats_batch(tiles)
            xn_war = p.count("spe")
            xn_tile_done = {}
            pend = []
            for t in tiles:
                xn_tile_done[t] = emit_xn_tile(
                    t, lambda c, t: xnc(c, 128 * t, 128 * (t + 1)), st_cnt,
                    war_pe=xn_war)
                pend.append((t, xn_tile_done[t]))

            def load_expert(e):
                g3, u3, d3 = wsrc(e)
                direct = (CALL_NO[0] == 0 and e < 2)
                load_matrix(0, g3, slot_free[0], None if direct else 3 * e)
                load_matrix(1, u3, slot_free[1], None if direct else 3 * e + 1)
                load_matrix(2, d3, slot_free[2], None if direct else 3 * e + 2)
            load_expert(0)
            drain_conversions(6)

            down_done_prev[0] = max(down_done_prev[0], ring_free_pe[0])
            for e in range(NE):
                gate_ready = slot_half_ready[0][:]
                up_ready = slot_half_ready[1][:]
                down_ready = slot_half_ready[2][1]
                hu_ready = {}
                last_gate_mm = None
                last_gu_mm = None
                for si, (s0, s1) in enumerate(slices):
                    n = s1 - s0
                    tdone = xn_tile_done[(s1 - 1) // 128]
                    p.wait("pe", "sx", tdone[0])
                    p.wait("pe", "sxa", tdone[1])
                    for m in range(8):
                        bkG = "A" if m % 2 == 0 else "C"
                        bkU = "B" if m % 2 == 0 else "D"
                        bs, bv = bank_free[bkG]
                        p.wait("pe", bs, bv)
                        for k in range(8):
                            p.wait("pe", "sw", gate_ready[k // 4])
                            fn = lambda t_, k=k, m=m, s0=s0, s1=s1, bkG=bkG: t_.matmul(
                                PS[bkG][:, 0:s1 - s0], slot_chunk(0, k, 128 * m, 128 * (m + 1)),
                                xnc(k, s0, s1), start=(k == 0), stop=(k == 7))
                            if k == 7:
                                gate_cnt = p.inc("pe", "spe", fn)
                            else:
                                p.raw("pe", ("r", fn))
                        last_gate_mm = gate_cnt
                        bs, bv = bank_free[bkU]
                        p.wait("pe", bs, bv)
                        for k in range(8):
                            p.wait("pe", "sw", up_ready[k // 4])
                            fn = lambda t_, k=k, m=m, s0=s0, s1=s1, bkU=bkU: t_.matmul(
                                PS[bkU][:, 0:s1 - s0], slot_chunk(1, k, 128 * m, 128 * (m + 1)),
                                xnc(k, s0, s1), start=(k == 0), stop=(k == 7))
                            if k == 7:
                                up_cnt = p.inc("pe", "spe", fn)
                            else:
                                p.raw("pe", ("r", fn))
                        last_gu_mm = up_cnt
                        if pend:
                            emit_routing(*pend.pop(0))
                        par = m % 2
                        p.wait("ac", "spe", gate_cnt)
                        p.wait("ac", "sdv", gs_free_ac[par])
                        silu_cnt = p.inc("ac", "sac", lambda s_, par=par, n=n, bkG=bkG:
                                         s_.activation(gs_sb[:, 512 * par: 512 * par + n],
                                                       PS[bkG][:, 0:n], AF.Silu))
                        gs_w_ac[par] = silu_cnt
                        bank_free[bkG] = ("sac", silu_cnt)
                        p.wait("dv", "spe", up_cnt)
                        p.wait("dv", "sac", silu_cnt)
                        p.wait("dv", "spe", down_done_prev[0])
                        hu_cnt = p.inc("dv", "sdv", lambda v, m=m, par=par, n=n, bkU=bkU:
                                       v.tensor_tensor(hu_sb[:, 512 * m: 512 * m + n],
                                                       gs_sb[:, 512 * par: 512 * par + n],
                                                       PS[bkU][:, 0:n], ALU.mult))
                        gs_free_ac[par] = hu_cnt
                        bank_free[bkU] = ("sdv", hu_cnt)
                        hu_ready[m] = hu_cnt
                    ntile = (n + 127) // 128
                    for mt in range(ntile):
                        tglob = (s0 // 128) + mt
                        bk0 = "E" if mt % 2 == 0 else "G"
                        bk1 = "F" if mt % 2 == 0 else "H"
                        p.wait("pe", "sw", down_ready)
                        bs, bv = bank_free[bk0]
                        p.wait("pe", bs, bv)
                        bs, bv = bank_free[bk1]
                        p.wait("pe", bs, bv)
                        for nn in range(2):
                            bk = bk0 if nn == 0 else bk1
                            for k in range(8):
                                p.wait("pe", "sdv", hu_ready[k])
                                fn = lambda t_, k=k, nn=nn, mt=mt, bk=bk: t_.matmul(
                                    PS[bk][:, 0:512],
                                    hu_sb[:, 512 * k + 128 * mt: 512 * k + 128 * (mt + 1)],
                                    slot_chunk(2, k, 512 * nn, 512 * (nn + 1)),
                                    start=(k == 0), stop=(k == 7))
                                if nn == 1 and k == 7:
                                    dn_cnt = p.inc("pe", "spe", fn)
                                else:
                                    p.raw("pe", ("r", fn))
                        down_done_prev[0] = dn_cnt
                        p.wait("dv", "spe", dn_cnt)
                        xs, xv = x_w[tglob]
                        p.wait("dv", xs, xv)
                        p.wait("dv", "sac", x_r_ac[tglob])
                        assert comb_w[tglob] > 0, f"routing for tile {tglob} not emitted yet"
                        p.wait("dv", "sdv", comb_w[tglob])
                        for nn in range(2):
                            bk = bk0 if nn == 0 else bk1
                            upd_cnt = p.inc("dv", "sdv", lambda v, bk=bk, nn=nn, tglob=tglob, e=e:
                                            v.scalar_tensor_tensor(
                                                x_sb[:, D * tglob + 512 * nn: D * tglob + 512 * (nn + 1)],
                                                PS[bk][:, 0:512],
                                                comb_sb[:, NE * tglob + e: NE * tglob + e + 1],
                                                x_sb[:, D * tglob + 512 * nn: D * tglob + 512 * (nn + 1)],
                                                ALU.mult, ALU.add))
                        x_w[tglob] = ("sdv", upd_cnt)
                        bank_free[bk0] = ("sdv", upd_cnt)
                        bank_free[bk1] = ("sdv", upd_cnt)
                slot_free[0] = last_gate_mm
                slot_free[1] = last_gu_mm
                slot_free[2] = down_done_prev[0]
                if e + 1 < NE:
                    load_expert(e + 1)
                drain_conversions(6)

        xn_tm_free = [(0, 0)]
        down_done_prev = [0]
        ln_ts_last = [0]
        ring_free_pe = [0]
        rt_score_read = [0, 0]
        CALL_NO = [0]
        comb_w = {t: 0 for t in range(NT)}

        def emit_attn():
            p.wait("gp", "spe", slot_free[0])
            p.wait("gp", "sw", conv_done[54])
            for h in range(2):
                wq_cnt = p.inc("gp", "sw", lambda g, h=h: g.dma_start(
                    out=xn_sb[:, 4096 * h: 4096 * (h + 1)],
                    in_=wbf[54][:, 4096 * h: 4096 * (h + 1)]), 16)
            load_matrix(0, wk_w, slot_free[0], 55)
            wk_cnt = slot_ready[0]
            load_matrix(1, wv_w, slot_free[1], 56)
            wv_cnt = slot_ready[1]
            load_matrix(2, wo_w, slot_free[2], 57)
            wo_cnt = slot_ready[2]

            ring_lastread = {s: 0 for s in range(5)}
            vring_lastread = {r: 0 for r in range(4)}
            ring_ready = {s: 0 for s in range(5)}
            vring_ready = {r: 0 for r in range(4)}
            ring_war_dv = p.count("sdv")
            xnr_lastread = {r: 0 for r in range(4)}
            prT_free = {0: 0, 1: 0, 2: 0}
            pr32_free = {0: 0, 1: 0, 2: 0}
            sc_free = {0: 0, 1: 0, 2: 0}
            attn_pe_start = [p.count("spe")]
            o_free = {cc: 0 for cc in range(8)}
            o_ev = {cc: 0 for cc in range(8)}
            q_ready_t = {}
            prep_done = {}
            head_nrm = {}
            head_exp = {}
            head_pt = {}
            pend_wo = [None]
            q_lastread_par = [0, 0]
            wlast = {"k": 0, "v": 0, "o": 0}

            def xnr(cc, r):
                return xn_sb[:, 8192 + 512 * cc + 128 * r:
                             8192 + 512 * cc + 128 * (r + 1)]

            st_cnt = emit_stats_batch(list(range(NT)))

            def prep(tau):
                r = tau % 4
                prep_done[tau] = emit_xn_tile(
                    tau, lambda cc, tt, r=r: xnr(cc, r), st_cnt,
                    queues=("sy",), war_pe=xnr_lastread[r])

            def mk_projK(tau, cc):
                def emit():
                    r = tau % 4
                    s1, s2 = tau % 5, (tau + 1) % 5
                    bk = "A" if cc % 2 == 0 else "B"
                    bs, bv = bank_free[bk]
                    p.wait("pe", bs, bv)
                    p.wait("pe", "sx", prep_done[tau][0])
                    p.wait("pe", "sx", prep_done[tau + 1][0])
                    p.wait("pe", "sw", wk_cnt)
                    for k in range(8):
                        fn = lambda t_, k=k, cc=cc, r=r, bk=bk: t_.matmul(
                            PS[bk][:, 0:256], slot_chunk(0, k, 128 * cc, 128 * (cc + 1)),
                            xn_sb[:, 8192 + 512 * k + 128 * r: 8192 + 512 * k + 128 * (r + 2)],
                            start=(k == 0), stop=(k == 7))
                        if k == 7:
                            kp_cnt = p.inc("pe", "spe", fn)
                        else:
                            p.raw("pe", ("r", fn))
                    xnr_lastread[r] = max(xnr_lastread[r], kp_cnt)
                    xnr_lastread[(tau + 1) % 4] = max(xnr_lastread[(tau + 1) % 4], kp_cnt)
                    wlast["k"] = kp_cnt
                    p.wait("ac", "spe", kp_cnt)
                    p.wait("ac", "spe", ring_lastread[s1])
                    p.wait("ac", "spe", ring_lastread[s2])
                    p.wait("ac", "sdv", ring_war_dv)
                    p.inc("ac", "sac", lambda s, cc=cc, s1=s1, bk=bk: s.add(
                        kr(cc, s1), PS[bk][:, 0:128], bcol[:, 8 + cc: 9 + cc]))
                    kv_cnt = p.inc("ac", "sac", lambda s, cc=cc, s2=s2, bk=bk: s.add(
                        kr(cc, s2), PS[bk][:, 128:256], bcol[:, 8 + cc: 9 + cc]))
                    bank_free[bk] = ("sac", kv_cnt)
                    ring_ready[s1] = max(ring_ready[s1], kv_cnt)
                    ring_ready[s2] = max(ring_ready[s2], kv_cnt)
                return emit

            def mk_projV(tau, nn):
                def emit():
                    r = tau % 4
                    bk = "C"
                    bs, bv = bank_free[bk]
                    p.wait("pe", bs, bv)
                    p.wait("pe", "sx", prep_done[tau][0])
                    p.wait("pe", "sw", wv_cnt)
                    for k in range(8):
                        fn = lambda t_, k=k, nn=nn, r=r, bk=bk: t_.matmul(
                            PS[bk][:, 0:512], xnr(k, r),
                            slot_chunk(1, k, 512 * nn, 512 * (nn + 1)),
                            start=(k == 0), stop=(k == 7))
                        if k == 7:
                            vp_cnt = p.inc("pe", "spe", fn)
                        else:
                            p.raw("pe", ("r", fn))
                    xnr_lastread[r] = max(xnr_lastread[r], vp_cnt)
                    wlast["v"] = vp_cnt
                    p.wait("ac", "spe", vp_cnt)
                    p.wait("ac", "spe", vring_lastread[r])
                    p.wait("ac", "sdv", ring_war_dv)
                    vv_cnt = p.inc("ac", "sac", lambda s, nn=nn, r=r, bk=bk: s.copy(
                        vr(r)[:, 512 * nn: 512 * (nn + 1)], PS[bk][:, 0:512]))
                    bank_free[bk] = ("sac", vv_cnt)
                    vring_ready[r] = max(vring_ready[r], vv_cnt)
                return emit

            def projQ(tau):
                r = tau % 4
                qev = [0, 0]
                for cc in range(8):
                    bk = "A" if cc % 2 == 0 else "B"
                    bs, bv = bank_free[bk]
                    p.wait("pe", bs, bv)
                    p.wait("pe", "sx", prep_done[tau][0])
                    p.wait("pe", "sx", prep_done[tau + 1][0])
                    p.wait("pe", "sw", wq_cnt)
                    for k in range(8):
                        fn = lambda t_, k=k, cc=cc, r=r, bk=bk: t_.matmul(
                            PS[bk][:, 0:256],
                            xn_sb[:, 1024 * k + 128 * cc: 1024 * k + 128 * (cc + 1)],
                            xn_sb[:, 8192 + 512 * k + 128 * r: 8192 + 512 * k + 128 * (r + 2)],
                            start=(k == 0), stop=(k == 7))
                        if k == 7:
                            qp_cnt = p.inc("pe", "spe", fn)
                        else:
                            p.raw("pe", ("r", fn))
                    xnr_lastread[r] = max(xnr_lastread[r], qp_cnt)
                    xnr_lastread[(tau + 1) % 4] = max(xnr_lastread[(tau + 1) % 4], qp_cnt)
                    p.wait("ac", "spe", qp_cnt)
                    for half in range(2):
                        tt = tau + half
                        if tt < 2 or tt >= NT:
                            continue
                        p.wait("ac", "spe", q_lastread_par[tt % 2])
                        qv_cnt = p.inc("ac", "sac", lambda s, cc=cc, tt=tt, half=half, bk=bk:
                                       s.add(q_buf(tt)[:, 128 * cc: 128 * (cc + 1)],
                                             PS[bk][:, 128 * half: 128 * (half + 1)],
                                             bcol[:, cc: cc + 1]))
                        qev[half] = qv_cnt
                    bank_free[bk] = ("sac", qv_cnt)
                for half in range(2):
                    tt = tau + half
                    if 2 <= tt < NT:
                        q_ready_t[tt] = max(qev)

            for tau in range(4):
                prep(tau)
            for it in ([mk_projK(0, cc) for cc in range(8)]
                       + [mk_projV(0, nn) for nn in range(2)]
                       + [mk_projV(1, nn) for nn in range(2)]
                       + [mk_projK(2, cc) for cc in range(8)]
                       + [mk_projV(2, nn) for nn in range(2)]
                       + [mk_projV(3, nn) for nn in range(2)]):
                it()
            projQ(2)

            for t in range(2, NT):
                if t + 2 < NT:
                    prep(t + 2)
                if t % 2 == 1 and t + 1 < NT:
                    feeds = ([mk_projK(t + 1, cc) for cc in range(8)]
                             + [mk_projV(t + 1, nn) for nn in range(2)]
                             + ([mk_projV(t + 2, nn) for nn in range(2)]
                                if t + 2 < NT else []))
                else:
                    feeds = []
                v_mask = min(t - 2, 2)
                slots = [(t - 2) % 5, (t - 1) % 5, t % 5]
                vslots = [(t - 2) % 4, (t - 1) % 4, t % 4]
                q_ready = q_ready_t[t]

                def head_front(h):
                    rho = h % 3
                    cc = h // 2
                    pe_off = (h % 2) * 64
                    bkS = BKS[rho]
                    bs, bv = bank_free[bkS]
                    p.wait("pe", bs, bv)
                    p.wait("pe", "sac", q_ready)
                    for j in range(3):
                        p.wait("pe", "sac", ring_ready[slots[j]])
                    for j in range(3):
                        fn = lambda t_, j=j, cc=cc, t=t, pe_off=pe_off, slots=slots, bkS=bkS: \
                            t_.matmul(
                                PS[bkS][:, 128 * j: 128 * (j + 1)],
                                q_buf(t)[pe_off: pe_off + 64, 128 * cc: 128 * (cc + 1)],
                                kr(cc, slots[j])[pe_off: pe_off + 64, :],
                                start=True, stop=True)
                        if j == 2:
                            sc_cnt = p.inc("pe", "spe", fn)
                        else:
                            p.raw("pe", ("r", fn))
                    q_lastread_par[t % 2] = sc_cnt
                    for j in range(3):
                        ring_lastread[slots[j]] = max(ring_lastread[slots[j]], sc_cnt)
                    p.wait("dv", "spe", sc_cnt)
                    p.wait("dv", "sac", sc_free[rho])
                    p.wait("dv", "sac", x_r_ac[t])
                    p.wait("dv", "sx", init_cnt)
                    if rho == 2:
                        p.wait("dv", "spe", attn_pe_start[0])
                    scm_cnt = p.inc("dv", "sdv", lambda v, rho=rho, v_mask=v_mask, bkS=bkS:
                                    v.scalar_tensor_tensor(
                                        sc_buf(rho), PS[bkS][:, 0:384], 0.125,
                                        mask_sb[:, 384 * v_mask: 384 * (v_mask + 1)],
                                        ALU.mult, ALU.add))
                    p.wait("dv", "sdv", scm_cnt)
                    nm_cnt = p.inc("dv", "sdv", lambda v, rho=rho: v.reduce_max(
                        st_sb[:, NMX[rho]: NMX[rho] + 1], sc_buf(rho),
                        axis=AXX, negate=True))
                    if rho < 2:
                        gs_free_ac[rho] = nm_cnt
                    p.wait("ac", "sdv", nm_cnt)
                    p.wait("ac", "spe", pr32_free[rho])
                    exp_cnt = p.inc("ac", "sac", lambda s, rho=rho: s.activation(
                        pr_buf(rho), sc_buf(rho), AF.Exp,
                        bias=st_sb[:, NMX[rho]: NMX[rho] + 1],
                        accum_out=st_sb[:, DEN[rho]: DEN[rho] + 1]))
                    sc_free[rho] = exp_cnt
                    head_exp[h] = exp_cnt

                def head_front_b(h):
                    rho = h % 3
                    exp_cnt = head_exp[h]
                    p.wait("dv", "sac", exp_cnt)
                    rc_cnt = p.inc("dv", "sdv", lambda v, rho=rho: v.reciprocal(
                        st_sb[:, RDE[rho]: RDE[rho] + 1], st_sb[:, DEN[rho]: DEN[rho] + 1]))
                    p.wait("dv", "sdv", rc_cnt)
                    nrm_cnt = p.inc("dv", "sdv", lambda v, rho=rho: v.tensor_scalar(
                        pr_buf(rho), pr_buf(rho),
                        st_sb[:, RDE[rho]: RDE[rho] + 1], None, ALU.mult))
                    head_nrm[h] = nrm_cnt

                def head_tail(h):
                    rho = h % 3
                    par = h % 2
                    cc = h // 2
                    pe_off = (h % 2) * 64
                    bkS = BKS[rho]
                    p.wait("pe", "sdv", head_nrm[h])
                    for j in range(3):
                        fn = lambda t_, j=j, rho=rho, bkS=bkS: t_.transpose(
                            PS[bkS][:, 128 * j: 128 * (j + 1)],
                            pr_buf(rho)[:, 128 * j: 128 * (j + 1)],
                            ident[:])
                        if j == 2:
                            trp_cnt = p.inc("pe", "spe", fn)
                        else:
                            p.raw("pe", ("r", fn))
                    pr32_free[rho] = trp_cnt
                    p.wait("dv", "spe", trp_cnt)
                    p.wait("dv", "spe", prT_free[rho])
                    pt_cnt = p.inc("dv", "sdv", lambda v, rho=rho, bkS=bkS: v.tensor_copy(
                        pt_buf(rho), PS[bkS][:, 0:384]))
                    bank_free[bkS] = ("sdv", pt_cnt)
                    head_pt[h] = pt_cnt

                def head_tail_b(h):
                    rho = h % 3
                    par = h % 2
                    cc = h // 2
                    pe_off = (h % 2) * 64
                    pt_cnt = head_pt[h]
                    bkO = "G" if cc % 2 == 0 else "H"
                    if par == 0:
                        bs, bv = bank_free[bkO]
                        p.wait("pe", bs, bv)
                    p.wait("pe", "sdv", pt_cnt)
                    for j in range(3):
                        p.wait("pe", "sac", vring_ready[vslots[j]])
                    for j in range(3):
                        fn = lambda t_, j=j, h=h, rho=rho, pe_off=pe_off, vslots=vslots, bkO=bkO: \
                            t_.matmul(PS[bkO][pe_off: pe_off + 64, 0:128],
                                      vr(vslots[j])[:, 64 * h: 64 * (h + 1)],
                                      pt_buf(rho)[:, 128 * j: 128 * (j + 1)],
                                      start=(j == 0), stop=(j == 2))
                        if j == 2:
                            av_cnt = p.inc("pe", "spe", fn)
                        else:
                            p.raw("pe", ("r", fn))
                    prT_free[rho] = av_cnt
                    for j in range(3):
                        vring_lastread[vslots[j]] = max(vring_lastread[vslots[j]], av_cnt)
                    if par == 1:
                        p.wait("ac", "spe", av_cnt)
                        p.wait("ac", "spe", o_free[cc])
                        oc_cnt = p.inc("ac", "sac", lambda s, cc=cc, bkO=bkO: s.add(
                            o_sb[:, 128 * cc: 128 * (cc + 1)], PS[bkO][:, 0:128],
                            bcol[:, 16 + cc: 17 + cc]))
                        bank_free[bkO] = ("sac", oc_cnt)
                        o_ev[cc] = oc_cnt

                for h in range(NH):
                    head_front(h)
                    if h == 1 and pend_wo[0] is not None:
                        pend_wo[0](0)
                    if h >= 1:
                        head_tail(h - 1)
                    head_front_b(h)
                    if h == 1 and pend_wo[0] is not None:
                        pend_wo[0](1)
                        pend_wo[0] = None
                    if h >= 1:
                        head_tail_b(h - 1)
                    if 6 <= h < 6 + len(feeds):
                        feeds[h - 6]()
                head_tail(NH - 1)
                head_tail_b(NH - 1)
                for it in feeds[10:]:
                    it()
                if t % 2 == 1 and t + 1 < NT:
                    projQ(t + 1)
                def mk_wo(t, o_ev_snap):
                    def emit(nn):
                        p.wait("pe", "sw", wo_cnt)
                        for cc in range(8):
                            p.wait("pe", "sac", o_ev_snap[cc])
                        bk = "C"
                        bs, bv = bank_free[bk]
                        p.wait("pe", bs, bv)
                        for k in range(8):
                            fn = lambda t_, k=k, nn=nn, bk=bk: t_.matmul(
                                PS[bk][:, 0:512], o_sb[:, 128 * k: 128 * (k + 1)],
                                slot_chunk(2, k, 512 * nn, 512 * (nn + 1)),
                                start=(k == 0), stop=False)
                            p.raw("pe", ("r", fn))
                        p.wait("pe", "sw", 32)
                        fn = lambda t_, nn=nn, bk=bk: t_.matmul(
                            PS[bk][:, 0:512], ones_bf[0:1, 0:128],
                            bo_sb[0:1, 512 * nn: 512 * (nn + 1)], start=False, stop=True)
                        wo_mm = p.inc("pe", "spe", fn)
                        wlast["o"] = wo_mm
                        p.wait("dv", "spe", wo_mm)
                        xs, xv = x_w[t]
                        p.wait("dv", xs, xv)
                        p.wait("dv", "sac", x_r_ac[t])
                        upd = p.inc("dv", "sdv", lambda v, t=t, nn=nn, bk=bk: v.tensor_add(
                            x_sb[:, D * t + 512 * nn: D * t + 512 * (nn + 1)],
                            x_sb[:, D * t + 512 * nn: D * t + 512 * (nn + 1)],
                            PS[bk][:, 0:512]))
                        bank_free[bk] = ("sdv", upd)
                        x_w[t] = ("sdv", upd)
                        if nn == 1:
                            for cc in range(8):
                                o_free[cc] = wo_mm
                    return emit
                pend_wo[0] = mk_wo(t, dict(o_ev))

            if pend_wo[0] is not None:
                pend_wo[0](0)
                pend_wo[0](1)
                pend_wo[0] = None
            slot_free[0] = wlast["k"]
            slot_free[1] = wlast["v"]
            slot_free[2] = wlast["o"]
            ring_free_pe[0] = p.count("spe")

        q_lastread = [0]
        ln_last_rstd = [0]

        for e_ in range(18):
            g_, u_, d_ = wsrc(e_)
            conv_queue.extend([(3 * e_, g_), (3 * e_ + 1, u_), (3 * e_ + 2, d_)])
        conv_queue.extend([(54, wq_w), (55, wk_w), (56, wv_w), (57, wo_w)])
        CALL_NO[0] = 0
        if phases & 1:
            emit_ffn(list(range(NT)))
        if phases & 2:
            emit_attn()
        CALL_NO[0] = 1
        if phases & 4:
            emit_ffn(list(range(2, NT)))

        for i in range(4):
            tiles = range(2 + 4 * i, 2 + 4 * (i + 1))
            for t in tiles:
                xs, xv = x_w[t]
                p.wait("sy", xs, xv)
            p.inc("sy", "sst", lambda sy, i=i: sy.dma_start(
                out=out.rearrange("(t p) d -> t p d", p=128)[4 * i: 4 * (i + 1)].transpose([1, 0, 2]),
                in_=x_sb[:, D * (2 + 4 * i): D * (2 + 4 * (i + 1))].rearrange(
                    "p (t d) -> p t d", t=4)), 16)
        p.wait("sy", "sst", p.count("sst"))

        def run(eng_obj, key):
            for item in p.ops[key]:
                if callable(item):
                    item(eng_obj)
                elif item[0] == "w":
                    eng_obj.wait_ge(SEM[item[1]], item[2])
                elif item[0] == "i":
                    item[3](eng_obj).then_inc(SEM[item[1]], item[2])
                elif item[0] == "r":
                    item[1](eng_obj)

        @block.tensor
        def _(t):
            run(t, "pe")

        @block.vector
        def _(v):
            run(v, "dv")

        @block.scalar
        def _(s):
            run(s, "ac")

        @block.gpsimd
        def _(g):
            run(g, "gp")

        @block.sync
        def _(sy):
            run(sy, "sy")

    return nc



_NC_CACHE = {}

def _get_program():
    if "nc" not in _NC_CACHE:
        _NC_CACHE["nc"] = build_program()
    return _NC_CACHE["nc"]


def _build_masks(is_start):
    i = np.arange(128)[:, None]
    m = np.empty((3, 128, 384), np.float32)
    c = np.arange(384)[None, :]
    band = (c > i) & (c <= i + 256)
    for v in range(3):
        allow = band.copy()
        if is_start and v < 2:
            allow &= c >= (256 - 128 * v)
        m[v] = np.where(allow, 0.0, NEG)
    return m


def _build_cmask():
    cm = np.zeros((128, 544), np.float32)
    i4 = np.arange(4)
    L4 = (i4[None, :] < i4[:, None]).astype(np.float32).reshape(-1)
    U4 = (i4[None, :] > i4[:, None]).astype(np.float32).reshape(-1)
    i16 = np.arange(16)
    L16 = (i16[None, :] < i16[:, None]).astype(np.float32).reshape(-1)
    U16 = (i16[None, :] > i16[:, None]).astype(np.float32).reshape(-1)
    cm[:, 0:16] = L4
    cm[:, 16:32] = U4
    cm[:, 32:288] = L16
    cm[:, 288:544] = U16
    return cm


def _wt(w):
    w = np.asarray(w, np.float32)
    return np.ascontiguousarray(w.reshape(8, 128, -1).transpose(1, 0, 2).reshape(128, -1))


def _wt3(w):
    w = np.asarray(w, np.float32)
    return np.ascontiguousarray(w.reshape(16, 8, 128, 1024).transpose(0, 2, 1, 3).reshape(16, 128, 8192))


def kernel(L0_attn_x, L0_attn_wq, L0_attn_bq, L0_attn_wk, L0_attn_bk,
           L0_attn_wv, L0_attn_bv, L0_attn_wo, L0_attn_bo,
           L0_ffn_router_w, L0_ffn_bias, L0_ffn_gate_w, L0_ffn_up_w,
           L0_ffn_down_w, L0_ffn_sh_gate_w, L0_ffn_sh_up_w, L0_ffn_sh_down_w):
    x = np.ascontiguousarray(np.asarray(L0_attn_x, np.float32).reshape(16384, D))
    bq = np.asarray(L0_attn_bq, np.float32)
    bk = np.asarray(L0_attn_bk, np.float32)
    bv = np.asarray(L0_attn_bv, np.float32)
    bqkv = np.concatenate([bq.reshape(8, 128).T, bk.reshape(8, 128).T,
                           bv.reshape(8, 128).T], axis=1).astype(np.float32)
    cmask = _build_cmask()
    rbias_b = np.broadcast_to(np.asarray(L0_ffn_bias, np.float32)[None, :],
                              (128, 16)).copy()
    identw = np.eye(128, dtype=np.float32)
    onesw = np.ones((1, 512), np.float32)
    shared = {
        "cmask": cmask, "identw": identw, "bqkv": bqkv, "rbias_b": rbias_b,
        "onesw": onesw, "rw": np.asarray(L0_ffn_router_w, np.float32),
        "bo_w": np.asarray(L0_attn_bo, np.float32).reshape(1, D),
        "wq_w": _wt(L0_attn_wq), "wk_w": _wt(L0_attn_wk),
        "wv_w": _wt(L0_attn_wv), "wo_w": _wt(L0_attn_wo),
        "gate_w": _wt3(L0_ffn_gate_w), "up_w": _wt3(L0_ffn_up_w),
        "down_w": _wt3(L0_ffn_down_w),
        "shg_w": np.stack([_wt(np.asarray(L0_ffn_sh_gate_w)[:, 1024 * h: 1024 * (h + 1)])
                           for h in range(2)]),
        "shu_w": np.stack([_wt(np.asarray(L0_ffn_sh_up_w)[:, 1024 * h: 1024 * (h + 1)])
                           for h in range(2)]),
        "shd_w": np.stack([_wt(np.asarray(L0_ffn_sh_down_w)[1024 * h: 1024 * (h + 1)])
                           for h in range(2)]),
    }
    in_maps = []
    for c in range(8):
        start = (c % 4 == 0)
        r0 = c * 2048
        if start:
            halo = np.zeros((HALO, D), np.float32)
        else:
            halo = x[r0 - HALO: r0]
        xin = np.concatenate([halo, x[r0: r0 + 2048]], 0)
        m = dict(shared)
        m["xin"] = np.ascontiguousarray(xin)
        m["masks"] = _build_masks(start)
        in_maps.append(m)
    nc = _get_program()
    res = run_bass_kernel_spmd(nc, in_maps, core_ids=list(range(8)))
    outp = np.concatenate([np.asarray(res.results[c]["out"], np.float32)
                           for c in range(8)], axis=0)
    return outp.reshape(2, 8192, D)
